# version 03
# baseline (speedup 1.0000x reference)
import contextlib
import numpy as np
import concourse.bass as bass
import concourse.mybir as mybir
from concourse.bass_utils import run_bass_kernel_spmd

F32 = mybir.dt.float32
AF = mybir.ActivationFunctionType
OP = mybir.AluOpType

D = 2048
HDIM = 8192
NH, NKV, GRP, DH = 16, 4, 4, 128
TOKC = 2048
TT = 256
NTILE = TOKC // TT
NSUB = TT // 128
NCHT = TT // 64
DC = D // 128
HC = HDIM // 128
NSC, SCW = 8, 8
EPS = 1e-5
INV_D = 1.0 / D

ENGS = ("sync", "tensor", "vector", "scalar", "gpsimd")


class _Ev:
    __slots__ = ("eng", "fn", "deps", "dma", "dkey", "needed", "order", "semk", "val")

    def __init__(self, eng, fn, deps, dma, dkey):
        self.eng = eng
        self.fn = fn
        self.deps = [d for d in deps if d is not None]
        self.dma = dma
        self.dkey = dkey
        self.needed = False
        self.order = -1
        self.semk = None
        self.val = None


class _Plan:
    def __init__(self):
        self.by = {e: [] for e in ENGS}
        self.dma_n = {e: 0 for e in ENGS}

    def ev(self, eng, fn, deps=(), dkey=None):
        dma = eng in ("sync", "gpsimd")
        assert dma == (dkey is not None)
        e = _Ev(eng, fn, deps, dma, dkey)
        e.order = len(self.by[eng])
        self.by[eng].append(e)
        return e

    def finalize(self):
        for eng in ENGS:
            for e in self.by[eng]:
                for d in e.deps:
                    if d.eng == e.eng:
                        assert d.order < e.order, f"same-engine dep out of order on {eng}"
                    else:
                        d.needed = True
        dcnt = {}
        for eng in ENGS:
            ncmp = 0
            for e in self.by[eng]:
                if e.dma:
                    dcnt[e.dkey] = dcnt.get(e.dkey, 0) + 1
                    e.semk = "D_" + e.dkey
                    e.val = dcnt[e.dkey] * 16
                elif e.needed:
                    ncmp += 1
                    e.semk = "P_" + eng
                    e.val = ncmp
        for eng in ENGS:
            for e in self.by[eng]:
                if e.dma and e.dkey == "c":
                    e.val = dcnt["c"] * 16
        self.dma_n = dcnt

    def run(self, eng, engobj, sems):
        waited = {}
        need_drain = eng in ("vector", "scalar")
        first = True
        for e in self.by[eng]:
            if need_drain and not first:
                engobj.drain()
            first = False
            for d in e.deps:
                if d.eng == eng:
                    continue
                assert d.semk is not None
                if waited.get(d.semk, 0) >= d.val:
                    continue
                engobj.wait_ge(sems[d.semk], d.val)
                waited[d.semk] = d.val
            ins = e.fn(engobj)
            if e.dma:
                ins.then_inc(sems[e.semk], 16)
            elif e.needed:
                ins.then_inc(sems[e.semk], 1)


def build_nc():
    nc = bass.Bass("TRN2", target_bir_lowering=False)
    dt = nc.dram_tensor
    x_d = dt("x", [TOKC, D], F32, kind="ExternalInput").ap()
    xp_d = dt("xp", [TOKC, D], F32, kind="ExternalInput").ap()
    stq_d = dt("stq", [NH, 128, D], F32, kind="ExternalInput").ap()
    stk_d = dt("stk", [NKV, 128, D], F32, kind="ExternalInput").ap()
    stv_d = dt("stv", [NKV, 128, D], F32, kind="ExternalInput").ap()
    sto_d = dt("sto", [DC, 128, D], F32, kind="ExternalInput").ap()
    stg_d = dt("stg", [HC, 128, D], F32, kind="ExternalInput").ap()
    stu_d = dt("stu", [HC, 128, D], F32, kind="ExternalInput").ap()
    std_d = dt("stdw", [NSC, DC, 128, SCW * 128], F32, kind="ExternalInput").ap()
    maskt_d = dt("maskt", [128, 2048], F32, kind="ExternalInput").ap()
    qdecb_d = dt("qdecb", [128, 1024], F32, kind="ExternalInput").ap()
    kdecx_d = dt("kdecx", [128, 32], F32, kind="ExternalInput").ap()
    blkb_d = dt("blkb", [128, 16], F32, kind="ExternalInput").ap()
    cosm_d = dt("cosm", [128, TOKC], F32, kind="ExternalInput").ap()
    sinm_d = dt("sinm", [128, TOKC], F32, kind="ExternalInput").ap()
    cosp_d = dt("cosp", [128, TOKC], F32, kind="ExternalInput").ap()
    sinp_d = dt("sinp", [128, TOKC], F32, kind="ExternalInput").ap()
    idm_d = dt("idm", [128, 128], F32, kind="ExternalInput").ap()
    epsc_d = dt("epsc", [128, 1], F32, kind="ExternalInput").ap()
    pmat_d = dt("pmat", [128, 128], F32, kind="ExternalInput").ap()
    y_d = dt("y", [TOKC, D], F32, kind="ExternalOutput").ap()

    with contextlib.ExitStack() as stk_ctx:
        E = stk_ctx.enter_context
        xbuf = E(nc.sbuf_tensor("xbuf", [128, 2 * D], F32))
        xnscr = E(nc.sbuf_tensor("xnscr", [128, D], F32))
        xnfm = E(nc.sbuf_tensor("xnfm", [128, DC * TT], F32))
        qfm = E(nc.sbuf_tensor("qfm", [128, NH * TT], F32))
        kfm = E(nc.sbuf_tensor("kfm", [128, NKV * TT], F32))
        vfm = E(nc.sbuf_tensor("vfm", [128, NKV * TT], F32))
        ktm = E(nc.sbuf_tensor("ktm", [128, NSUB * 512], F32))
        vtm = E(nc.sbuf_tensor("vtm", [128, NSUB * 512], F32))
        ofm = E(nc.sbuf_tensor("ofm", [128, NH * TT], F32))
        kv_s = E(nc.sbuf_tensor("kvs", [128, NH * 128], F32))
        gated = E(nc.sbuf_tensor("gated", [128, SCW * TT], F32))
        yacc = E(nc.sbuf_tensor("yacc", [128, DC * TT], F32))
        wa = E(nc.sbuf_tensor("wa", [128, 3 * D], F32))
        wd = E(nc.sbuf_tensor("wdb", [128, 2 * SCW * 128], F32))
        ropefm = E(nc.sbuf_tensor("ropefm", [128, 2 * TT], F32))
        projscr = E(nc.sbuf_tensor("projscr", [128, 2 * TT], F32))
        ffnscr = E(nc.sbuf_tensor("ffnscr", [128, 2 * TT], F32))
        sTscr = E(nc.sbuf_tensor("sTscr", [128, 2 * 64], F32))
        qtscr = E(nc.sbuf_tensor("qtscr", [128, 2 * 64], F32))
        ktscr = E(nc.sbuf_tensor("ktscr", [128, 2 * 128], F32))
        statc = E(nc.sbuf_tensor("statc", [128, 16], F32))
        maskt_s = E(nc.sbuf_tensor("maskts", [128, 2048], F32))
        qdecb_s = E(nc.sbuf_tensor("qdecbs", [128, 1024], F32))
        kdecx_s = E(nc.sbuf_tensor("kdecxs", [128, 32], F32))
        blkb_s = E(nc.sbuf_tensor("blkbs", [128, 16], F32))
        idm_s = E(nc.sbuf_tensor("idms", [128, 128], F32))
        pmat_s = E(nc.sbuf_tensor("pmats", [128, 128], F32))
        eps_s = E(nc.sbuf_tensor("epss", [128, 1], F32))
        PB = [E(nc.psum_tensor(f"pb{i}", [128, 512], F32)) for i in range(8)]
        sems = {}
        DKEYS = ("c", "wa0", "wa1", "wa2", "wd0", "wd1", "xb0", "xb1",
                 "rc", "rs", "o0", "o1")
        for k in ("P_tensor", "P_vector", "P_scalar") + tuple("D_" + d for d in DKEYS):
            sems[k] = E(nc.semaphore(k))

        P = _Plan()
        bank_writer = [None] * 8
        bank_readers = [[] for _ in range(8)]

        def dma(dst, src, key, deps=()):
            def fn(eng, d=dst, s=src):
                return eng.dma_start(out=d, in_=s)
            return P.ev("sync", fn, deps, dkey=key)

        def mm(b, out, l, r, start, stop, deps=()):
            def fn(eng, o=out, ll=l, rr=r, st=start, sp=stop):
                return eng.matmul(o, ll, rr, start=st, stop=sp)
            e = P.ev("tensor", fn, deps)
            bank_writer[b] = e
            return e

        def mmgroup(b, out, pairs, deps=()):
            d0 = list(deps) + bank_readers[b]
            bank_readers[b] = []
            last = None
            n = len(pairs)
            for i, (l, r) in enumerate(pairs):
                last = mm(b, out, l, r, i == 0, i == n - 1, d0 if i == 0 else ())
            return last

        def transpose(b, out, in_, deps=()):
            d0 = list(deps) + bank_readers[b]
            bank_readers[b] = []

            def fn(eng, o=out, i=in_):
                return eng.transpose(o, i, idm_s[:])
            e = P.ev("tensor", fn, d0)
            bank_writer[b] = e
            return e

        def rd_bank(b, e):
            bank_readers[b].append(e)
            return e

        def vcopy(dst, src, deps=()):
            def fn(eng, d=dst, s=src):
                return eng.tensor_copy(d, s)
            return P.ev("vector", fn, deps)

        def scopy(dst, src, deps=()):
            def fn(eng, d=dst, s=src):
                return eng.copy(d, s)
            return P.ev("scalar", fn, deps)

        def vtt(dst, a, b_, op, deps=()):
            def fn(eng, d=dst, x=a, y=b_, o=op):
                return eng.tensor_tensor(d, x, y, o)
            return P.ev("vector", fn, deps)

        def vts(dst, in0, s1, s2, op0, op1, deps=()):
            def fn(eng, d=dst, i0=in0, a=s1, b2=s2, o0=op0, o1=op1):
                if b2 is None:
                    return eng.tensor_scalar(out=d, in0=i0, scalar1=a, scalar2=None, op0=o0)
                return eng.tensor_scalar(out=d, in0=i0, scalar1=a, scalar2=b2, op0=o0, op1=o1)
            return P.ev("vector", fn, deps)

        def vstt(dst, in0, scal, in1, op0, op1, deps=()):
            def fn(eng, d=dst, i0=in0, s=scal, i1=in1, o0=op0, o1=op1):
                return eng.scalar_tensor_tensor(out=d, in0=i0, scalar=s, in1=i1, op0=o0, op1=o1)
            return P.ev("vector", fn, deps)

        def vred(dst, src, deps=()):
            def fn(eng, d=dst, s=src):
                return eng.tensor_reduce(out=d, in_=s, axis=mybir.AxisListType.X, op=OP.add)
            return P.ev("vector", fn, deps)

        def vrecip(dst, src, deps=()):
            def fn(eng, d=dst, s=src):
                return eng.reciprocal(d, s)
            return P.ev("vector", fn, deps)

        def sact(dst, src, func, deps=(), bias=None, scale=None, accum=None):
            def fn(eng, d=dst, s=src, f=func, b2=bias, sc=scale, ac=accum):
                kw = {}
                if b2 is not None:
                    kw["bias"] = b2
                if sc is not None:
                    kw["scale"] = sc
                if ac is not None:
                    kw["accum_out"] = ac
                return eng.activation(out=d, in_=s, func=f, **kw)
            return P.ev("scalar", fn, deps)

        def gdma(dst, src, key, deps=()):
            def fn(eng, d=dst, s=src):
                return eng.dma_start(out=d, in_=s)
            return P.ev("gpsimd", fn, deps, dkey=key)

        ev_maskt = dma(maskt_s[:], maskt_d, "c")
        ev_qdecb = dma(qdecb_s[:], qdecb_d, "c")
        ev_kdecx = dma(kdecx_s[:], kdecx_d, "c")
        ev_blkb = dma(blkb_s[:], blkb_d, "c")
        ev_idm = dma(idm_s[:], idm_d, "c")
        ev_pmat = dma(pmat_s[:], pmat_d, "c")
        ev_eps = dma(eps_s[:], epsc_d, "c")
        ev_kv = []
        for h in range(NH):
            def fnz(eng, h=h):
                return eng.memset(kv_s[:, h * 128:(h + 1) * 128], 0.0)
            ev_kv.append(P.ev("vector", fnz))

        wa_slots = [None, None, None]
        wa_i = [0]
        wd_slots = [None, None]
        wd_i = [0]
        rope_cons = []
        xbuf_free = [[], []]
        tr_alt = [0]
        st_reader = [None, None]
        qt_reader = [None, None]
        kt_reader = [None, None]
        ffn_reader = [None, None]
        state = {
            "xnfm_consumed": None,
            "attn_t_last": None,
            "comb_t_last": None,
            "sg_last": None,
            "xnscr_free": None,
        }

        def wa_load(src, deps=()):
            slot = wa_i[0] % 3
            wa_i[0] += 1
            ap = wa[:, slot * D:(slot + 1) * D]
            e = dma(ap, src, f"wa{slot}",
                    list(deps) + ([wa_slots[slot]] if wa_slots[slot] is not None else []))
            return slot, ap, e

        def wd_load(src, deps=()):
            slot = wd_i[0] % 2
            wd_i[0] += 1
            ap = wd[:, slot * 1024:(slot + 1) * 1024]
            e = dma(ap, src, f"wd{slot}",
                    list(deps) + ([wd_slots[slot]] if wd_slots[slot] is not None else []))
            return slot, ap, e

        def do_tile(ti, prelude):
            t0 = ti * TT
            xsrc = xp_d if prelude else x_d
            cossrc = cosp_d if prelude else cosm_d
            sinsrc = sinp_d if prelude else sinm_d

            old_cons = list(rope_cons)
            rope_cons.clear()
            ev_cos = dma(ropefm[:, 0:TT], cossrc[:, t0:t0 + TT], "rc", old_cons)
            ev_sin = dma(ropefm[:, TT:2 * TT], sinsrc[:, t0:t0 + TT], "rs", old_cons)

            xn_prev = state["xnfm_consumed"]
            last_evac = None
            for s in range(NSUB):
                xb = xbuf[:, s * D:(s + 1) * D]
                ev_x = dma(xb, xsrc[t0 + s * 128: t0 + s * 128 + 128, :], f"xb{s}", xbuf_free[s])
                c0 = statc[:, s * 8 + 0: s * 8 + 1]
                c1 = statc[:, s * 8 + 1: s * 8 + 2]
                c2 = statc[:, s * 8 + 2: s * 8 + 3]
                c3 = statc[:, s * 8 + 3: s * 8 + 4]
                c4 = statc[:, s * 8 + 4: s * 8 + 5]
                c5 = statc[:, s * 8 + 5: s * 8 + 6]
                ev_red = vred(c0, xb, [ev_x])
                vts(c1, c0, INV_D, None, OP.mult, None)
                ev_musq = vtt(c3, c1, c1, OP.mult)
                ev_ssq = sact(xnscr[:], xb, AF.Square,
                              deps=[ev_x, state["xnscr_free"]], accum=c2)
                ev_var = vts(c4, c2, INV_D, None, OP.mult, None, [ev_ssq])
                ev_var2 = vtt(c4, c4, c3, OP.subtract)
                ev_std = sact(c5, c4, AF.Sqrt, deps=[ev_var2, ev_eps], bias=eps_s[:, 0:1])
                ev_rstd = vrecip(c4, c5, [ev_std])
                ev_xn = vts(xnscr[:], xb, c1, c4, OP.subtract, OP.mult, [ev_rstd])
                xbuf_free[s] = [ev_ssq, ev_xn]
                last_tr = None
                for dc in range(DC):
                    b = 4 + (tr_alt[0] % 2)
                    tr_alt[0] += 1
                    evt = transpose(b, PB[b][:, 0:128], xnscr[:, dc * 128:(dc + 1) * 128],
                                    [ev_xn, ev_idm])
                    eve = scopy(xnfm[:, dc * TT + s * 128: dc * TT + s * 128 + 128],
                                PB[b][:, 0:128], [evt, xn_prev])
                    rd_bank(b, eve)
                    last_tr = evt
                    last_evac = eve
                state["xnscr_free"] = last_tr
            ev_xn_ready = last_evac

            pr_alt = [0]
            attn_prev = state["attn_t_last"]
            comb_prev = state["comb_t_last"]

            def proj(nout, wsrc, dstbuf, rope, tmbuf):
                finals = []
                tm_last = None
                for oc in range(nout):
                    slot, wap, ev_w = wa_load(wsrc[oc])
                    b = pr_alt[0] % 2
                    pr_alt[0] += 1
                    pairs = [(wap[:, dc * 128:(dc + 1) * 128], xnfm[:, dc * TT:(dc + 1) * TT])
                             for dc in range(DC)]
                    ev_mm = mmgroup(b, PB[b][:, 0:TT], pairs, [ev_w, ev_xn_ready])
                    wa_slots[slot] = ev_mm
                    dst = dstbuf[:, oc * TT:(oc + 1) * TT]
                    ev_raw = rd_bank(b, vcopy(dst, PB[b][:, 0:TT],
                                              [ev_mm, attn_prev, comb_prev]))
                    if rope:
                        rb = 2 + (pr_alt[0] % 2)
                        d0 = list(bank_readers[rb]) + [ev_raw, ev_pmat]
                        bank_readers[rb] = []
                        ev_rot = mm(rb, PB[rb][:, 0:TT], pmat_s[:], dst, True, True, d0)
                        ev_t0 = rd_bank(rb, vtt(projscr[:, 0:TT], PB[rb][:, 0:TT],
                                                ropefm[:, TT:2 * TT], OP.mult,
                                                [ev_rot, ev_sin, ev_cos]))
                        ev_t1 = vtt(projscr[:, TT:2 * TT], dst, ropefm[:, 0:TT], OP.mult)
                        ev_rp = vtt(dst, projscr[:, TT:2 * TT], projscr[:, 0:TT], OP.add)
                        rope_cons.extend([ev_t0, ev_t1])
                        final = ev_rp
                    else:
                        final = ev_raw
                    finals.append(final)
                    if tmbuf is not None:
                        for s in range(NSUB):
                            b2 = 4 + (tr_alt[0] % 2)
                            tr_alt[0] += 1
                            evt = transpose(b2, PB[b2][:, 0:128],
                                            dstbuf[:, oc * TT + s * 128: oc * TT + s * 128 + 128],
                                            [final, attn_prev])
                            eve = scopy(tmbuf[:, s * 512 + oc * 128: s * 512 + oc * 128 + 128],
                                        PB[b2][:, 0:128], [evt, attn_prev])
                            rd_bank(b2, eve)
                            tm_last = eve
                return finals, tm_last

            ev_k, ev_ktm = proj(NKV, stk_d, kfm, True, ktm)
            ev_v, ev_vtm = proj(NKV, stv_d, vfm, False, vtm)
            if not prelude:
                ev_q, _ = proj(NH, stq_d, qfm, True, None)

            ya_last = [None] * DC
            down_last_mm = [None]
            xn_last_mm_box = [None]
            gated_evs_sc = [[]]
            sg_prev = state["sg_last"]

            def emit_gateup_half(hgl, half):
                b = hgl % 2
                hcl = hgl % SCW
                if half == 0:
                    slotg, wgp, ev_wg = wa_load(stg_d[hgl])
                    pairs = [(wgp[:, dc * 128:(dc + 1) * 128], xnfm[:, dc * TT:(dc + 1) * TT])
                             for dc in range(DC)]
                    ev_gm = mmgroup(b, PB[b][:, 0:TT], pairs, [ev_wg, ev_xn_ready])
                    wa_slots[slotg] = ev_gm
                    emit_gateup_half.gm = ev_gm
                else:
                    ev_gm = emit_gateup_half.gm
                    slotu, wup, ev_wu = wa_load(stu_d[hgl])
                    pairs = [(wup[:, dc * 128:(dc + 1) * 128], xnfm[:, dc * TT:(dc + 1) * TT])
                             for dc in range(DC)]
                    d0 = [ev_wu]
                    ev_um = mm(b, PB[b][:, TT:2 * TT], pairs[0][0], pairs[0][1], True, False, d0)
                    for i in range(1, DC):
                        ev_um = mm(b, PB[b][:, TT:2 * TT], pairs[i][0], pairs[i][1],
                                   False, i == DC - 1, ())
                    wa_slots[slotu] = ev_um
                    xn_last_mm_box[0] = ev_um
                    fsl = hgl % 2
                    ev_si = rd_bank(b, sact(ffnscr[:, fsl * TT:(fsl + 1) * TT],
                                            PB[b][:, 0:TT], AF.Silu,
                                            [ev_gm, ev_um, ffn_reader[fsl]]))
                    ev_gt = rd_bank(b, vtt(gated[:, hcl * TT:(hcl + 1) * TT],
                                           ffnscr[:, fsl * TT:(fsl + 1) * TT],
                                           PB[b][:, TT:2 * TT], OP.mult,
                                           [ev_si, down_last_mm[0]]))
                    ffn_reader[fsl] = ev_gt
                    gated_evs_sc[0].append(ev_gt)

            def emit_down_sc(sc):
                gevs = gated_evs_sc[0]
                last = None
                for oc in range(DC):
                    slotd, wdp, ev_wd = wd_load(std_d[sc, oc])
                    yb = 4 + oc % 2
                    pairs = [(wdp[:, hcl * 128:(hcl + 1) * 128],
                              gated[:, hcl * TT:(hcl + 1) * TT]) for hcl in range(SCW)]
                    ev_ym = mmgroup(yb, PB[yb][:, 0:TT], pairs, [ev_wd] + gevs)
                    wd_slots[slotd] = ev_ym
                    if sc == 0:
                        ev_ya = scopy(yacc[:, oc * TT:(oc + 1) * TT], PB[yb][:, 0:TT],
                                      [ev_ym, sg_prev])
                    else:
                        ev_ya = vtt(yacc[:, oc * TT:(oc + 1) * TT],
                                    yacc[:, oc * TT:(oc + 1) * TT], PB[yb][:, 0:TT],
                                    OP.add, [ev_ym, ya_last[oc]])
                    rd_bank(yb, ev_ya)
                    ya_last[oc] = ev_ya
                    last = ev_ym
                down_last_mm[0] = last
                gated_evs_sc[0] = []

            attn_t_last = None
            o_evac_last = None
            kidx = 0
            for c in range(NCHT):
                s, par = c // 2, c % 2
                for h in range(NH):
                    g = h // GRP
                    idx = (c * NH + h) % 2
                    ab = 6 + idx
                    if not prelude:
                        d0 = list(bank_readers[ab]) + [ev_k[g], ev_q[h]]
                        bank_readers[ab] = []
                        ev_sc = mm(ab, PB[ab][:, 128:192],
                                   kfm[:, g * TT + s * 128: g * TT + s * 128 + 128],
                                   qfm[:, h * TT + c * 64: h * TT + c * 64 + 64],
                                   True, True, d0)
                        ev_st = rd_bank(ab, vtt(sTscr[:, idx * 64:(idx + 1) * 64],
                                                PB[ab][:, 128:192],
                                                maskt_s[:, par * 1024 + h * 64: par * 1024 + h * 64 + 64],
                                                OP.mult, [ev_sc, ev_maskt, st_reader[idx]]))
                        ev_qt = vtt(qtscr[:, idx * 64:(idx + 1) * 64],
                                    qfm[:, h * TT + c * 64: h * TT + c * 64 + 64],
                                    qdecb_s[:, h * 64:(h + 1) * 64], OP.mult,
                                    [ev_q[h], ev_qdecb, qt_reader[idx]])
                        emit_gateup_half(kidx, 0)
                        ob = 2 + idx
                        d1 = list(bank_readers[ob]) + [ev_st, ev_vtm]
                        bank_readers[ob] = []
                        ev_o1 = mm(ob, PB[ob][:, 0:64],
                                   vtm[:, s * 512 + g * 128: s * 512 + g * 128 + 128],
                                   sTscr[:, idx * 64:(idx + 1) * 64], True, False, d1)
                        ev_o2 = mm(ob, PB[ob][:, 0:64],
                                   kv_s[:, h * 128:(h + 1) * 128],
                                   qtscr[:, idx * 64:(idx + 1) * 64], False, True,
                                   [ev_qt, ev_kv[h]])
                        ev_oe = rd_bank(ob, scopy(ofm[:, h * TT + c * 64: h * TT + c * 64 + 64],
                                                  PB[ob][:, 0:64], [ev_o2, sg_prev]))
                        st_reader[idx] = ev_o1
                        qt_reader[idx] = ev_o2
                        o_evac_last = ev_oe
                        emit_gateup_half(kidx, 1)
                    ev_kt = vts(ktscr[:, idx * 128:(idx + 1) * 128],
                                ktm[:, s * 512 + g * 128: s * 512 + g * 128 + 128],
                                kdecx_s[:, par * 16 + h: par * 16 + h + 1], None,
                                OP.mult, None, [ev_ktm, ev_kdecx, kt_reader[idx]])
                    d2 = list(bank_readers[ab]) + [ev_kt, ev_vtm]
                    if not prelude:
                        d2.append(ev_st)
                    bank_readers[ab] = []
                    ev_A = mm(ab, PB[ab][:, 0:128],
                              ktscr[:, idx * 128:(idx + 1) * 128],
                              vtm[:, s * 512 + g * 128: s * 512 + g * 128 + 128],
                              True, True, d2)
                    kvdeps = [ev_A, ev_kv[h], ev_blkb]
                    if not prelude:
                        kvdeps.append(ev_o2)
                    ev_kvu = rd_bank(ab, vstt(kv_s[:, h * 128:(h + 1) * 128],
                                              kv_s[:, h * 128:(h + 1) * 128],
                                              blkb_s[:, h:h + 1], PB[ab][:, 0:128],
                                              OP.mult, OP.add, kvdeps))
                    ev_kv[h] = ev_kvu
                    kt_reader[idx] = ev_A
                    attn_t_last = ev_A
                    if not prelude:
                        if kidx % SCW == SCW - 1:
                            emit_down_sc(kidx // SCW)
                        kidx += 1
            state["attn_t_last"] = attn_t_last

            if prelude:
                state["xnfm_consumed"] = attn_t_last
                return

            ev_s = []
            for oc in range(DC):
                slot, wap, ev_w = wa_load(sto_d[oc])
                b = pr_alt[0] % 2
                pr_alt[0] += 1
                pairs = [(wap[:, ec * 128:(ec + 1) * 128], ofm[:, ec * TT:(ec + 1) * TT])
                         for ec in range(DC)]
                ev_mm = mmgroup(b, PB[b][:, 0:TT], pairs, [ev_w, o_evac_last])
                wa_slots[slot] = ev_mm
                ev_se = rd_bank(b, vcopy(qfm[:, oc * TT:(oc + 1) * TT], PB[b][:, 0:TT],
                                         [ev_mm, attn_t_last]))
                ev_s.append(ev_se)

            state["xnfm_consumed"] = xn_last_mm_box[0]

            ev_xr = []
            for s in range(NSUB):
                ev_xr.append(dma(xbuf[:, s * D:(s + 1) * D],
                                 xsrc[t0 + s * 128: t0 + s * 128 + 128, :],
                                 f"xb{s}", xbuf_free[s]))
            sg_last = None
            add_last = [None, None]
            for oc in range(DC):
                ev_g = sact(yacc[:, oc * TT:(oc + 1) * TT], yacc[:, oc * TT:(oc + 1) * TT],
                            AF.Silu, [ya_last[oc]])
                ev_sg = vtt(qfm[:, oc * TT:(oc + 1) * TT], qfm[:, oc * TT:(oc + 1) * TT],
                            yacc[:, oc * TT:(oc + 1) * TT], OP.mult, [ev_g, ev_s[oc]])
                sg_last = ev_sg
                for s in range(NSUB):
                    b = 4 + (tr_alt[0] % 2)
                    tr_alt[0] += 1
                    evt = transpose(b, PB[b][:, 0:128],
                                    qfm[:, oc * TT + s * 128: oc * TT + s * 128 + 128],
                                    [ev_sg])
                    ev_add = rd_bank(b, vtt(xbuf[:, s * D + oc * 128: s * D + oc * 128 + 128],
                                            xbuf[:, s * D + oc * 128: s * D + oc * 128 + 128],
                                            PB[b][:, 0:128], OP.add, [evt, ev_xr[s]]))
                    add_last[s] = ev_add
                    state["comb_t_last"] = evt
            state["sg_last"] = sg_last
            for s in range(NSUB):
                ev_out = gdma(y_d[t0 + s * 128: t0 + s * 128 + 128, :],
                              xbuf[:, s * D:(s + 1) * D], f"o{s}", [add_last[s]])
                xbuf_free[s] = [ev_out]

        for ti in range(NTILE):
            do_tile(ti, prelude=True)
        for ti in range(NTILE):
            do_tile(ti, prelude=False)

        P.finalize()
        block = E(nc.Block())

        @block.sync
        def _(eng):
            P.run("sync", eng, sems)

        @block.tensor
        def _(eng):
            P.run("tensor", eng, sems)

        @block.vector
        def _(eng):
            P.run("vector", eng, sems)

        @block.scalar
        def _(eng):
            P.run("scalar", eng, sems)

        @block.gpsimd
        def _(eng):
            P.run("gpsimd", eng, sems)
            eng.wait_ge(sems["D_o0"], P.dma_n["o0"] * 16)
            eng.wait_ge(sems["D_o1"], P.dma_n["o1"] * 16)

    return nc


def _tables():
    import jax
    import jax.numpy as jnp
    cpu = jax.devices("cpu")[0]
    with jax.default_device(cpu):
        i = jnp.arange(1, NH + 1, dtype=jnp.float32)
        lam = jnp.exp2(-8.0 * i / NH)
        r = jnp.arange(64, dtype=jnp.float32)
        diff = r[:, None] - r[None, :]
        diag = jnp.where(diff >= 0, jnp.exp(-lam[:, None, None] * diff), 0.0)
        q_dec = jnp.exp(-lam[:, None] * (r + 1.0))
        k_dec = jnp.exp(-lam[:, None] * (64 - 1.0 - r))
        blk = jnp.exp(-lam * 64)
        inv_freq = 1.0 / 10000.0 ** (jnp.arange(0, DH, 2, dtype=jnp.float32) / DH)
        freqs = jnp.arange(4096, dtype=jnp.float32)[:, None] * inv_freq[None, :]
        cos = jnp.concatenate([jnp.cos(freqs), jnp.cos(freqs)], axis=-1)
        sin = jnp.concatenate([jnp.sin(freqs), jnp.sin(freqs)], axis=-1)
        out = [np.asarray(t) for t in (diag, q_dec, k_dec, blk, cos, sin)]
    return out


def _retile(w, nout):
    no, din = w.shape
    assert no == nout * 128
    ndc = din // 128
    return np.ascontiguousarray(
        w.reshape(nout, 128, ndc, 128).transpose(0, 3, 2, 1)).reshape(nout, 128, din)


_NC_CACHE = {}


def kernel(L0_attn_x, L0_attn_wq, L0_attn_wk, L0_attn_wv, L0_attn_wo,
           L0_ffn_w_gate, L0_ffn_w_up, L0_ffn_w_down):
    x = np.ascontiguousarray(np.asarray(L0_attn_x, dtype=np.float32)).reshape(8 * TOKC, D)
    wq = np.asarray(L0_attn_wq, dtype=np.float32)
    wk = np.asarray(L0_attn_wk, dtype=np.float32)
    wv = np.asarray(L0_attn_wv, dtype=np.float32)
    wo = np.asarray(L0_attn_wo, dtype=np.float32)
    wg = np.asarray(L0_ffn_w_gate, dtype=np.float32)
    wu = np.asarray(L0_ffn_w_up, dtype=np.float32)
    wdn = np.asarray(L0_ffn_w_down, dtype=np.float32)

    stq = _retile(wq, NH)
    stk = _retile(wk, NKV)
    stv = _retile(wv, NKV)
    sto = _retile(wo, DC)
    stg = _retile(wg, HC)
    stu = _retile(wu, HC)
    stdw = np.ascontiguousarray(
        wdn.reshape(DC, 128, NSC, SCW, 128).transpose(2, 0, 4, 3, 1)
    ).reshape(NSC, DC, 128, SCW * 128)

    diag, q_dec, k_dec, blk, cos, sin = _tables()
    maskt = np.zeros((128, 2048), np.float32)
    for par in range(2):
        for h in range(NH):
            maskt[par * 64:(par + 1) * 64, par * 1024 + h * 64: par * 1024 + (h + 1) * 64] = diag[h].T
    qdecb = np.zeros((128, 1024), np.float32)
    for h in range(NH):
        qdecb[:, h * 64:(h + 1) * 64] = q_dec[h][None, :]
    kdecx = np.zeros((128, 32), np.float32)
    for par in range(2):
        for h in range(NH):
            kdecx[par * 64:(par + 1) * 64, par * 16 + h] = k_dec[h]
    blkb = np.broadcast_to(blk[None, :], (128, 16)).astype(np.float32).copy()
    idm = np.eye(128, dtype=np.float32)
    pmat = np.zeros((128, 128), np.float32)
    for d in range(64):
        pmat[d, d + 64] = 1.0
        pmat[d + 64, d] = -1.0
    cosp = np.ascontiguousarray(cos[0:TOKC].T)
    sinp = np.ascontiguousarray(sin[0:TOKC].T)

    if "nc" not in _NC_CACHE:
        _NC_CACHE["nc"] = build_nc()
    nc = _NC_CACHE["nc"]

    zeros_x = np.zeros((TOKC, D), np.float32)
    in_maps = []
    for c in range(8):
        half = c % 2
        b = c // 2
        xm = x[c * TOKC:(c + 1) * TOKC]
        xpm = x[b * 2 * TOKC: b * 2 * TOKC + TOKC] if half == 1 else zeros_x
        p0 = half * TOKC
        in_maps.append(dict(
            x=xm, xp=xpm, stq=stq, stk=stk, stv=stv, sto=sto, stg=stg, stu=stu,
            stdw=stdw, maskt=maskt, qdecb=qdecb, kdecx=kdecx, blkb=blkb,
            cosm=np.ascontiguousarray(cos[p0:p0 + TOKC].T),
            sinm=np.ascontiguousarray(sin[p0:p0 + TOKC].T),
            cosp=cosp, sinp=sinp, idm=idm, pmat=pmat,
            epsc=np.full((128, 1), 1e-5, np.float32),
        ))
    res = run_bass_kernel_spmd(nc, in_maps, core_ids=list(range(8)))
    y = np.concatenate([np.asarray(res.results[c]["y"]) for c in range(8)], axis=0)
    return y.reshape(4, 4096, 2048).astype(np.float32)


if __name__ == "__main__":
    nc = build_nc()
    print("built ok")


# version 04
# speedup vs baseline: 1.0562x; 1.0562x over previous
import contextlib
import numpy as np
import concourse.bass as bass
import concourse.mybir as mybir
from concourse.bass_utils import run_bass_kernel_spmd

F32 = mybir.dt.float32
AF = mybir.ActivationFunctionType
OP = mybir.AluOpType

D = 2048
HDIM = 8192
NH, NKV, GRP, DH = 16, 4, 4, 128
TOKC = 2048
TT = 256
NTILE = TOKC // TT
NSUB = TT // 128
NCHT = TT // 64
DC = D // 128
HC = HDIM // 128
NSC, SCW = 8, 8
EPS = 1e-5
INV_D = 1.0 / D

ENGS = ("sync", "tensor", "vector", "scalar", "gpsimd")


class _Ev:
    __slots__ = ("eng", "fn", "deps", "dma", "dkey", "needed", "order", "semk", "val")

    def __init__(self, eng, fn, deps, dma, dkey):
        self.eng = eng
        self.fn = fn
        self.deps = [d for d in deps if d is not None]
        self.dma = dma
        self.dkey = dkey
        self.needed = False
        self.order = -1
        self.semk = None
        self.val = None


class _Plan:
    def __init__(self):
        self.by = {e: [] for e in ENGS}
        self.dma_n = {e: 0 for e in ENGS}

    def ev(self, eng, fn, deps=(), dkey=None):
        dma = eng in ("sync", "gpsimd")
        assert dma == (dkey is not None)
        e = _Ev(eng, fn, deps, dma, dkey)
        e.order = len(self.by[eng])
        self.by[eng].append(e)
        return e

    def finalize(self):
        for eng in ENGS:
            for e in self.by[eng]:
                for d in e.deps:
                    if d.eng == e.eng:
                        assert d.order < e.order, f"same-engine dep out of order on {eng}"
                    else:
                        d.needed = True
        dcnt = {}
        for eng in ENGS:
            ncmp = 0
            for e in self.by[eng]:
                if e.dma:
                    dcnt[e.dkey] = dcnt.get(e.dkey, 0) + 1
                    e.semk = "D_" + e.dkey
                    e.val = dcnt[e.dkey] * 16
                elif e.needed:
                    ncmp += 1
                    e.semk = "P_" + eng
                    e.val = ncmp
        for eng in ENGS:
            for e in self.by[eng]:
                if e.dma and e.dkey == "c":
                    e.val = dcnt["c"] * 16
        self.dma_n = dcnt

    def run(self, eng, engobj, sems):
        waited = {}
        need_drain = eng in ("vector", "scalar")
        first = True
        for e in self.by[eng]:
            if need_drain and not first:
                engobj.drain()
            first = False
            for d in e.deps:
                if d.eng == eng:
                    continue
                assert d.semk is not None
                if waited.get(d.semk, 0) >= d.val:
                    continue
                engobj.wait_ge(sems[d.semk], d.val)
                waited[d.semk] = d.val
            ins = e.fn(engobj)
            if e.dma:
                ins.then_inc(sems[e.semk], 16)
            elif e.needed:
                ins.then_inc(sems[e.semk], 1)


def build_nc():
    nc = bass.Bass("TRN2", target_bir_lowering=False)
    dt = nc.dram_tensor
    x_d = dt("x", [TOKC, D], F32, kind="ExternalInput").ap()
    xp_d = dt("xp", [TOKC, D], F32, kind="ExternalInput").ap()
    stq_d = dt("stq", [NH, 128, D], F32, kind="ExternalInput").ap()
    stk_d = dt("stk", [NKV, 128, D], F32, kind="ExternalInput").ap()
    stv_d = dt("stv", [NKV, 128, D], F32, kind="ExternalInput").ap()
    sto_d = dt("sto", [DC, 128, D], F32, kind="ExternalInput").ap()
    stg_d = dt("stg", [HC, 128, D], F32, kind="ExternalInput").ap()
    stu_d = dt("stu", [HC, 128, D], F32, kind="ExternalInput").ap()
    std_d = dt("stdw", [NSC, DC, 128, SCW * 128], F32, kind="ExternalInput").ap()
    maskt_d = dt("maskt", [128, 2048], F32, kind="ExternalInput").ap()
    qdecb_d = dt("qdecb", [128, 1024], F32, kind="ExternalInput").ap()
    kdecx_d = dt("kdecx", [128, 32], F32, kind="ExternalInput").ap()
    blkb_d = dt("blkb", [128, 16], F32, kind="ExternalInput").ap()
    cosm_d = dt("cosm", [128, TOKC], F32, kind="ExternalInput").ap()
    sinm_d = dt("sinm", [128, TOKC], F32, kind="ExternalInput").ap()
    cosp_d = dt("cosp", [128, TOKC], F32, kind="ExternalInput").ap()
    sinp_d = dt("sinp", [128, TOKC], F32, kind="ExternalInput").ap()
    idm_d = dt("idm", [128, 128], F32, kind="ExternalInput").ap()
    epsc_d = dt("epsc", [128, 1], F32, kind="ExternalInput").ap()
    pmat_d = dt("pmat", [128, 128], F32, kind="ExternalInput").ap()
    y_d = dt("y", [TOKC, D], F32, kind="ExternalOutput").ap()
    BF = mybir.dt.bfloat16
    hq_d = dt("hq", [NH, 128, 2 * D], BF, kind="Internal").ap()
    hk_d = dt("hk", [NKV, 128, 2 * D], BF, kind="Internal").ap()
    hv_d = dt("hv", [NKV, 128, 2 * D], BF, kind="Internal").ap()
    ho_d = dt("ho", [DC, 128, 2 * D], BF, kind="Internal").ap()
    hg_d = dt("hg", [HC, 128, 2 * D], BF, kind="Internal").ap()
    hu_d = dt("hu", [HC, 128, 2 * D], BF, kind="Internal").ap()
    hd_d = dt("hd", [NSC, DC, 128, 2 * SCW * 128], BF, kind="Internal").ap()

    with contextlib.ExitStack() as stk_ctx:
        E = stk_ctx.enter_context
        xbuf = E(nc.sbuf_tensor("xbuf", [128, 2 * D], F32))
        xnscr = E(nc.sbuf_tensor("xnscr", [128, D], F32))
        BF = mybir.dt.bfloat16
        xnh = E(nc.sbuf_tensor("xnh", [128, DC * TT], BF))
        xnl = E(nc.sbuf_tensor("xnl", [128, DC * TT], BF))
        qfm = E(nc.sbuf_tensor("qfm", [128, NH * TT], F32))
        kfm = E(nc.sbuf_tensor("kfm", [128, NKV * TT], F32))
        vfm = E(nc.sbuf_tensor("vfm", [128, NKV * TT], F32))
        ktm = E(nc.sbuf_tensor("ktm", [128, NSUB * 512], F32))
        vtm = E(nc.sbuf_tensor("vtm", [128, NSUB * 512], F32))
        oh = E(nc.sbuf_tensor("oh", [128, NH * TT], BF))
        ol = E(nc.sbuf_tensor("ol", [128, NH * TT], BF))
        kv_s = E(nc.sbuf_tensor("kvs", [128, NH * 128], F32))
        gth = E(nc.sbuf_tensor("gth", [128, SCW * TT], BF))
        gtl = E(nc.sbuf_tensor("gtl", [128, SCW * TT], BF))
        yacc = E(nc.sbuf_tensor("yacc", [128, DC * TT], F32))
        wa = E(nc.sbuf_tensor("wa", [128, 2 * D], F32))
        wab = E(nc.sbuf_tensor("wab", [128, 2 * 2 * D], BF))
        wd = E(nc.sbuf_tensor("wdf", [128, 2 * SCW * 128], F32))
        wdb = E(nc.sbuf_tensor("wdbb", [128, 2 * 2 * SCW * 128], BF))
        ropefm = E(nc.sbuf_tensor("ropefm", [128, 2 * TT], F32))
        projscr = E(nc.sbuf_tensor("projscr", [128, 2 * TT], F32))
        ffnscr = E(nc.sbuf_tensor("ffnscr", [128, 2 * TT], F32))
        sTscr = E(nc.sbuf_tensor("sTscr", [128, 2 * 64], F32))
        qtscr = E(nc.sbuf_tensor("qtscr", [128, 2 * 64], F32))
        ktscr = E(nc.sbuf_tensor("ktscr", [128, 2 * 128], F32))
        statc = E(nc.sbuf_tensor("statc", [128, 16], F32))
        maskt_s = E(nc.sbuf_tensor("maskts", [128, 2048], F32))
        qdecb_s = E(nc.sbuf_tensor("qdecbs", [128, 1024], F32))
        kdecx_s = E(nc.sbuf_tensor("kdecxs", [128, 32], F32))
        blkb_s = E(nc.sbuf_tensor("blkbs", [128, 16], F32))
        idm_s = E(nc.sbuf_tensor("idms", [128, 128], F32))
        pmat_s = E(nc.sbuf_tensor("pmats", [128, 128], F32))
        eps_s = E(nc.sbuf_tensor("epss", [128, 1], F32))
        PB = [E(nc.psum_tensor(f"pb{i}", [128, 512], F32)) for i in range(8)]
        sems = {}
        DKEYS = ("c", "wa0", "wa1", "wd0", "wd1", "xb0", "xb1",
                 "rc", "rs", "o0", "o1", "wab0", "wab1", "wdb0", "wdb1",
                 "swa0", "swa1", "swd0", "swd1")
        for k in ("P_tensor", "P_vector", "P_scalar") + tuple("D_" + d for d in DKEYS):
            sems[k] = E(nc.semaphore(k))

        P = _Plan()
        bank_writer = [None] * 8
        bank_readers = [[] for _ in range(8)]

        def dma(dst, src, key, deps=()):
            def fn(eng, d=dst, s=src):
                return eng.dma_start(out=d, in_=s)
            return P.ev("sync", fn, deps, dkey=key)

        def mm(b, out, l, r, start, stop, deps=()):
            def fn(eng, o=out, ll=l, rr=r, st=start, sp=stop):
                return eng.matmul(o, ll, rr, start=st, stop=sp)
            e = P.ev("tensor", fn, deps)
            bank_writer[b] = e
            return e

        def mmgroup(b, out, pairs, deps=()):
            d0 = list(deps) + bank_readers[b]
            bank_readers[b] = []
            last = None
            n = len(pairs)
            for i, (l, r) in enumerate(pairs):
                last = mm(b, out, l, r, i == 0, i == n - 1, d0 if i == 0 else ())
            return last

        def transpose(b, out, in_, deps=()):
            d0 = list(deps) + bank_readers[b]
            bank_readers[b] = []

            def fn(eng, o=out, i=in_):
                return eng.transpose(o, i, idm_s[:])
            e = P.ev("tensor", fn, d0)
            bank_writer[b] = e
            return e

        def rd_bank(b, e):
            bank_readers[b].append(e)
            return e

        def vcopy(dst, src, deps=()):
            def fn(eng, d=dst, s=src):
                return eng.tensor_copy(d, s)
            return P.ev("vector", fn, deps)

        def scopy(dst, src, deps=()):
            def fn(eng, d=dst, s=src):
                return eng.copy(d, s)
            return P.ev("scalar", fn, deps)

        def vtt(dst, a, b_, op, deps=()):
            def fn(eng, d=dst, x=a, y=b_, o=op):
                return eng.tensor_tensor(d, x, y, o)
            return P.ev("vector", fn, deps)

        def vts(dst, in0, s1, s2, op0, op1, deps=()):
            def fn(eng, d=dst, i0=in0, a=s1, b2=s2, o0=op0, o1=op1):
                if b2 is None:
                    return eng.tensor_scalar(out=d, in0=i0, scalar1=a, scalar2=None, op0=o0)
                return eng.tensor_scalar(out=d, in0=i0, scalar1=a, scalar2=b2, op0=o0, op1=o1)
            return P.ev("vector", fn, deps)

        def vstt(dst, in0, scal, in1, op0, op1, deps=()):
            def fn(eng, d=dst, i0=in0, s=scal, i1=in1, o0=op0, o1=op1):
                return eng.scalar_tensor_tensor(out=d, in0=i0, scalar=s, in1=i1, op0=o0, op1=o1)
            return P.ev("vector", fn, deps)

        def vred(dst, src, deps=()):
            def fn(eng, d=dst, s=src):
                return eng.tensor_reduce(out=d, in_=s, axis=mybir.AxisListType.X, op=OP.add)
            return P.ev("vector", fn, deps)

        def vrecip(dst, src, deps=()):
            def fn(eng, d=dst, s=src):
                return eng.reciprocal(d, s)
            return P.ev("vector", fn, deps)

        def sact(dst, src, func, deps=(), bias=None, scale=None, accum=None):
            def fn(eng, d=dst, s=src, f=func, b2=bias, sc=scale, ac=accum):
                kw = {}
                if b2 is not None:
                    kw["bias"] = b2
                if sc is not None:
                    kw["scale"] = sc
                if ac is not None:
                    kw["accum_out"] = ac
                return eng.activation(out=d, in_=s, func=f, **kw)
            return P.ev("scalar", fn, deps)

        def gdma(dst, src, key, deps=()):
            def fn(eng, d=dst, s=src):
                return eng.dma_start(out=d, in_=s)
            return P.ev("gpsimd", fn, deps, dkey=key)

        ev_maskt = dma(maskt_s[:], maskt_d, "c")
        ev_qdecb = dma(qdecb_s[:], qdecb_d, "c")
        ev_kdecx = dma(kdecx_s[:], kdecx_d, "c")
        ev_blkb = dma(blkb_s[:], blkb_d, "c")
        ev_idm = dma(idm_s[:], idm_d, "c")
        ev_pmat = dma(pmat_s[:], pmat_d, "c")
        ev_eps = dma(eps_s[:], epsc_d, "c")
        ev_kv = []
        for h in range(NH):
            def fnz(eng, h=h):
                return eng.memset(kv_s[:, h * 128:(h + 1) * 128], 0.0)
            ev_kv.append(P.ev("vector", fnz))

        wa_slots = [None, None]
        wa_i = [0]
        wab_slots = [[], []]
        wab_i = [0]
        wd_slots = [None, None]
        wd_i = [0]
        wdb_slots = [[], []]
        wdb_i = [0]
        store_last = {}
        split_done = [None]
        rope_cons = []
        xbuf_free = [[], []]
        tr_alt = [0]
        st_reader = [None, None]
        qt_reader = [None, None]
        kt_reader = [None, None]
        ffn_reader = [None, None]
        prod_reader = [None, None]
        state = {
            "xnfm_consumed": None,
            "attn_t_last": None,
            "comb_t_last": None,
            "sg_last": None,
            "xnscr_free": None,
        }

        def wa_load(src, deps=()):
            slot = wa_i[0] % 2
            wa_i[0] += 1
            ap = wa[:, slot * D:(slot + 1) * D]
            e = dma(ap, src, f"wa{slot}",
                    list(deps) + ([wa_slots[slot]] if wa_slots[slot] is not None else []))
            return slot, ap, e

        def get_wpair(srcf32, hl_d, oc, mode):
            if mode == "load":
                slot = wab_i[0] % 2
                wab_i[0] += 1
                ap = wab[:, slot * 2 * D:(slot + 1) * 2 * D]
                e = dma(ap, hl_d[oc], f"wab{slot}",
                        wab_slots[slot] + list(split_done[0] or ()))
                wab_slots[slot] = []
                hold = [e]
            else:
                slotf, wfp, ev_wf = wa_load(srcf32[oc])
                slot = wab_i[0] % 2
                wab_i[0] += 1
                ap = wab[:, slot * 2 * D:(slot + 1) * 2 * D]
                hi, lo = ap[:, 0:D], ap[:, D:2 * D]
                ev_hi = sact(hi, wfp, AF.Copy, deps=[ev_wf] + wab_slots[slot])
                ev_lo = vtt(lo, wfp, hi, OP.subtract, [ev_hi])
                wa_slots[slotf] = ev_lo
                wab_slots[slot] = []
                hold = [ev_hi, ev_lo]
                if mode == "split_store":
                    g0 = P.ev("gpsimd", (lambda eng, d=hl_d[oc][:, 0:D], s=hi:
                                         eng.dma_start(out=d, in_=s)),
                              [ev_hi, ev_lo], dkey=f"swa{slot}")
                    g1 = P.ev("gpsimd", (lambda eng, d=hl_d[oc][:, D:2 * D], s=lo:
                                         eng.dma_start(out=d, in_=s)),
                              [ev_lo], dkey=f"swa{slot}")
                    wab_slots[slot] += [g1]
                    store_last[f"swa{slot}"] = g1
            return ap[:, 0:D], ap[:, D:2 * D], hold, slot

        def wd_load(src, deps=()):
            slot = wd_i[0] % 2
            wd_i[0] += 1
            ap = wd[:, slot * 1024:(slot + 1) * 1024]
            e = dma(ap, src, f"wd{slot}",
                    list(deps) + ([wd_slots[slot]] if wd_slots[slot] is not None else []))
            return slot, ap, e

        def get_dpair(sc, oc, mode):
            W = SCW * 128
            if mode == "load":
                slot = wdb_i[0] % 2
                wdb_i[0] += 1
                ap = wdb[:, slot * 2 * W:(slot + 1) * 2 * W]
                e = dma(ap, hd_d[sc, oc], f"wdb{slot}",
                        wdb_slots[slot] + list(split_done[0] or ()))
                wdb_slots[slot] = []
                return ap[:, 0:W], ap[:, W:2 * W], [e], slot
            slotf, wfp, ev_wf = wd_load(std_d[sc, oc])
            slot = wdb_i[0] % 2
            wdb_i[0] += 1
            ap = wdb[:, slot * 2 * W:(slot + 1) * 2 * W]
            hi, lo = ap[:, 0:W], ap[:, W:2 * W]
            ev_hi = sact(hi, wfp, AF.Copy, deps=[ev_wf] + wdb_slots[slot])
            ev_lo = vtt(lo, wfp, hi, OP.subtract, [ev_hi])
            wd_slots[slotf] = ev_lo
            wdb_slots[slot] = []
            g0 = P.ev("gpsimd", (lambda eng, d=hd_d[sc, oc][:, 0:W], s=hi:
                                 eng.dma_start(out=d, in_=s)),
                      [ev_hi, ev_lo], dkey=f"swd{slot}")
            g1 = P.ev("gpsimd", (lambda eng, d=hd_d[sc, oc][:, W:2 * W], s=lo:
                                 eng.dma_start(out=d, in_=s)),
                      [ev_lo], dkey=f"swd{slot}")
            wdb_slots[slot] += [g1]
            store_last[f"swd{slot}"] = g1
            return hi, lo, [ev_hi, ev_lo], slot

        def mm3group(b, out, quads, deps=()):
            d0 = list(deps) + bank_readers[b]
            bank_readers[b] = []
            last = None
            n = len(quads)
            for i, (whi, wlo, xhi, xlo) in enumerate(quads):
                last = mm(b, out, whi, xhi, i == 0, False, d0 if i == 0 else ())
                last = mm(b, out, whi, xlo, False, False, ())
                last = mm(b, out, wlo, xhi, False, i == n - 1, ())
            return last

        def do_tile(ti, prelude):
            wmode = "split" if prelude else ("split_store" if ti == 0 else "load")
            t0 = ti * TT
            xsrc = xp_d if prelude else x_d
            cossrc = cosp_d if prelude else cosm_d
            sinsrc = sinp_d if prelude else sinm_d

            old_cons = list(rope_cons)
            rope_cons.clear()
            ev_cos = dma(ropefm[:, 0:TT], cossrc[:, t0:t0 + TT], "rc", old_cons)
            ev_sin = dma(ropefm[:, TT:2 * TT], sinsrc[:, t0:t0 + TT], "rs", old_cons)

            xn_prev = state["xnfm_consumed"]
            last_evac = None
            for s in range(NSUB):
                xb = xbuf[:, s * D:(s + 1) * D]
                ev_x = dma(xb, xsrc[t0 + s * 128: t0 + s * 128 + 128, :], f"xb{s}", xbuf_free[s])
                c0 = statc[:, s * 8 + 0: s * 8 + 1]
                c1 = statc[:, s * 8 + 1: s * 8 + 2]
                c2 = statc[:, s * 8 + 2: s * 8 + 3]
                c3 = statc[:, s * 8 + 3: s * 8 + 4]
                c4 = statc[:, s * 8 + 4: s * 8 + 5]
                c5 = statc[:, s * 8 + 5: s * 8 + 6]
                ev_red = vred(c0, xb, [ev_x])
                vts(c1, c0, INV_D, None, OP.mult, None)
                ev_musq = vtt(c3, c1, c1, OP.mult)
                ev_ssq = sact(xnscr[:], xb, AF.Square,
                              deps=[ev_x, state["xnscr_free"]], accum=c2)
                ev_var = vts(c4, c2, INV_D, None, OP.mult, None, [ev_ssq])
                ev_var2 = vtt(c4, c4, c3, OP.subtract)
                ev_std = sact(c5, c4, AF.Sqrt, deps=[ev_var2, ev_eps], bias=eps_s[:, 0:1])
                ev_rstd = vrecip(c4, c5, [ev_std])
                ev_xn = vts(xnscr[:], xb, c1, c4, OP.subtract, OP.mult, [ev_rstd])
                xbuf_free[s] = [ev_ssq, ev_xn]
                last_tr = None
                for dc in range(DC):
                    b = 4 + (tr_alt[0] % 2)
                    tr_alt[0] += 1
                    evt = transpose(b, PB[b][:, 0:128], xnscr[:, dc * 128:(dc + 1) * 128],
                                    [ev_xn, ev_idm])
                    co = dc * TT + s * 128
                    ehi = scopy(xnh[:, co:co + 128], PB[b][:, 0:128], [evt, xn_prev])
                    elo = vtt(xnl[:, co:co + 128], PB[b][:, 0:128], xnh[:, co:co + 128],
                              OP.subtract, [ehi, xn_prev])
                    rd_bank(b, ehi)
                    rd_bank(b, elo)
                    last_tr = evt
                    last_evac = [ehi, elo]
                state["xnscr_free"] = last_tr
            ev_xn_ready = last_evac

            pr_alt = [0]
            attn_prev = state["attn_t_last"]
            comb_prev = state["comb_t_last"]

            def proj(nout, wsrc, hlsrc, dstbuf, rope, tmbuf):
                finals = []
                tm_last = None
                for oc in range(nout):
                    whi, wlo, hold, bslot = get_wpair(wsrc, hlsrc, oc, wmode)
                    b = pr_alt[0] % 2
                    pr_alt[0] += 1
                    quads = [(whi[:, dc * 128:(dc + 1) * 128], wlo[:, dc * 128:(dc + 1) * 128],
                              xnh[:, dc * TT:(dc + 1) * TT], xnl[:, dc * TT:(dc + 1) * TT])
                             for dc in range(DC)]
                    ev_mm = mm3group(b, PB[b][:, 0:TT], quads, hold + ev_xn_ready)
                    wab_slots[bslot].append(ev_mm)
                    dst = dstbuf[:, oc * TT:(oc + 1) * TT]
                    ev_raw = rd_bank(b, vcopy(dst, PB[b][:, 0:TT],
                                              [ev_mm, attn_prev, comb_prev]))
                    if rope:
                        rb = 2 + (pr_alt[0] % 2)
                        d0 = list(bank_readers[rb]) + [ev_raw, ev_pmat]
                        bank_readers[rb] = []
                        ev_rot = mm(rb, PB[rb][:, 0:TT], pmat_s[:], dst, True, True, d0)
                        ev_t0 = rd_bank(rb, vtt(projscr[:, 0:TT], PB[rb][:, 0:TT],
                                                ropefm[:, TT:2 * TT], OP.mult,
                                                [ev_rot, ev_sin, ev_cos]))
                        ev_t1 = vtt(projscr[:, TT:2 * TT], dst, ropefm[:, 0:TT], OP.mult)
                        ev_rp = vtt(dst, projscr[:, TT:2 * TT], projscr[:, 0:TT], OP.add)
                        rope_cons.extend([ev_t0, ev_t1])
                        final = ev_rp
                    else:
                        final = ev_raw
                    finals.append(final)
                    if tmbuf is not None:
                        for s in range(NSUB):
                            b2 = 4 + (tr_alt[0] % 2)
                            tr_alt[0] += 1
                            evt = transpose(b2, PB[b2][:, 0:128],
                                            dstbuf[:, oc * TT + s * 128: oc * TT + s * 128 + 128],
                                            [final, attn_prev])
                            eve = scopy(tmbuf[:, s * 512 + oc * 128: s * 512 + oc * 128 + 128],
                                        PB[b2][:, 0:128], [evt, attn_prev])
                            rd_bank(b2, eve)
                            tm_last = eve
                return finals, tm_last

            ev_k, ev_ktm = proj(NKV, stk_d, hk_d, kfm, True, ktm)
            ev_v, ev_vtm = proj(NKV, stv_d, hv_d, vfm, False, vtm)
            if not prelude:
                ev_q, _ = proj(NH, stq_d, hq_d, qfm, True, None)

            ya_last = [None] * DC
            down_last_mm = [None]
            xn_last_mm_box = [None]
            gated_evs_sc = [[]]
            sg_prev = state["sg_last"]

            def emit_gateup_half(hgl, half):
                b = hgl % 2
                hcl = hgl % SCW
                if half == 0:
                    whi, wlo, hold, bslot = get_wpair(stg_d, hg_d, hgl, wmode)
                    quads = [(whi[:, dc * 128:(dc + 1) * 128], wlo[:, dc * 128:(dc + 1) * 128],
                              xnh[:, dc * TT:(dc + 1) * TT], xnl[:, dc * TT:(dc + 1) * TT])
                             for dc in range(DC)]
                    ev_gm = mm3group(b, PB[b][:, 0:TT], quads, hold + ev_xn_ready)
                    wab_slots[bslot].append(ev_gm)
                    emit_gateup_half.gm = ev_gm
                else:
                    ev_gm = emit_gateup_half.gm
                    whi, wlo, hold, bslot = get_wpair(stu_d, hu_d, hgl, wmode)
                    quads = [(whi[:, dc * 128:(dc + 1) * 128], wlo[:, dc * 128:(dc + 1) * 128],
                              xnh[:, dc * TT:(dc + 1) * TT], xnl[:, dc * TT:(dc + 1) * TT])
                             for dc in range(DC)]
                    d0 = list(hold)
                    ev_um = None
                    for i, (a1, a2, a3, a4) in enumerate(quads):
                        ev_um = mm(b, PB[b][:, TT:2 * TT], a1, a3, i == 0, False,
                                   d0 if i == 0 else ())
                        ev_um = mm(b, PB[b][:, TT:2 * TT], a1, a4, False, False, ())
                        ev_um = mm(b, PB[b][:, TT:2 * TT], a2, a3, False, i == DC - 1, ())
                    wab_slots[bslot].append(ev_um)
                    xn_last_mm_box[0] = ev_um
                    fsl = hgl % 2
                    ev_si = rd_bank(b, sact(ffnscr[:, fsl * TT:(fsl + 1) * TT],
                                            PB[b][:, 0:TT], AF.Silu,
                                            [ev_gm, ev_um, ffn_reader[fsl]]))
                    ev_pr = rd_bank(b, vtt(projscr[:, fsl * TT:(fsl + 1) * TT],
                                           ffnscr[:, fsl * TT:(fsl + 1) * TT],
                                           PB[b][:, TT:2 * TT], OP.mult,
                                           [ev_si, down_last_mm[0], prod_reader[fsl]]))
                    co = hcl * TT
                    ev_gh = sact(gth[:, co:co + TT], projscr[:, fsl * TT:(fsl + 1) * TT],
                                 AF.Copy, deps=[ev_pr, down_last_mm[0]])
                    ev_gl = vtt(gtl[:, co:co + TT], projscr[:, fsl * TT:(fsl + 1) * TT],
                                gth[:, co:co + TT], OP.subtract, [ev_gh, down_last_mm[0]])
                    prod_reader[fsl] = ev_gl
                    ffn_reader[fsl] = ev_pr
                    gated_evs_sc[0].extend([ev_gh, ev_gl])

            def emit_down_sc(sc):
                gevs = gated_evs_sc[0]
                last = None
                for oc in range(DC):
                    dhi, dlo, hold, bslot = get_dpair(sc, oc, wmode if wmode != "split" else "split_store")
                    yb = 4 + oc % 2
                    quads = [(dhi[:, hcl * 128:(hcl + 1) * 128], dlo[:, hcl * 128:(hcl + 1) * 128],
                              gth[:, hcl * TT:(hcl + 1) * TT], gtl[:, hcl * TT:(hcl + 1) * TT])
                             for hcl in range(SCW)]
                    ev_ym = mm3group(yb, PB[yb][:, 0:TT], quads, hold + gevs)
                    wdb_slots[bslot].append(ev_ym)
                    if sc == 0:
                        ev_ya = scopy(yacc[:, oc * TT:(oc + 1) * TT], PB[yb][:, 0:TT],
                                      [ev_ym, sg_prev])
                    else:
                        ev_ya = vtt(yacc[:, oc * TT:(oc + 1) * TT],
                                    yacc[:, oc * TT:(oc + 1) * TT], PB[yb][:, 0:TT],
                                    OP.add, [ev_ym, ya_last[oc]])
                    rd_bank(yb, ev_ya)
                    ya_last[oc] = ev_ya
                    last = ev_ym
                down_last_mm[0] = last
                gated_evs_sc[0] = []

            attn_t_last = None
            o_evac_last = None
            kidx = 0
            for c in range(NCHT):
                s, par = c // 2, c % 2
                for h in range(NH):
                    g = h // GRP
                    idx = (c * NH + h) % 2
                    ab = 6 + idx
                    if not prelude:
                        d0 = list(bank_readers[ab]) + [ev_k[g], ev_q[h]]
                        bank_readers[ab] = []
                        ev_sc = mm(ab, PB[ab][:, 128:192],
                                   kfm[:, g * TT + s * 128: g * TT + s * 128 + 128],
                                   qfm[:, h * TT + c * 64: h * TT + c * 64 + 64],
                                   True, True, d0)
                        ev_st = rd_bank(ab, vtt(sTscr[:, idx * 64:(idx + 1) * 64],
                                                PB[ab][:, 128:192],
                                                maskt_s[:, par * 1024 + h * 64: par * 1024 + h * 64 + 64],
                                                OP.mult, [ev_sc, ev_maskt, st_reader[idx]]))
                        ev_qt = vtt(qtscr[:, idx * 64:(idx + 1) * 64],
                                    qfm[:, h * TT + c * 64: h * TT + c * 64 + 64],
                                    qdecb_s[:, h * 64:(h + 1) * 64], OP.mult,
                                    [ev_q[h], ev_qdecb, qt_reader[idx]])
                        emit_gateup_half(kidx, 0)
                        ob = 2 + idx
                        d1 = list(bank_readers[ob]) + [ev_st, ev_vtm]
                        bank_readers[ob] = []
                        ev_o1 = mm(ob, PB[ob][:, 0:64],
                                   vtm[:, s * 512 + g * 128: s * 512 + g * 128 + 128],
                                   sTscr[:, idx * 64:(idx + 1) * 64], True, False, d1)
                        ev_o2 = mm(ob, PB[ob][:, 0:64],
                                   kv_s[:, h * 128:(h + 1) * 128],
                                   qtscr[:, idx * 64:(idx + 1) * 64], False, True,
                                   [ev_qt, ev_kv[h]])
                        co2 = h * TT + c * 64
                        ev_oh = scopy(oh[:, co2:co2 + 64], PB[ob][:, 0:64], [ev_o2, sg_prev])
                        ev_ol = vtt(ol[:, co2:co2 + 64], PB[ob][:, 0:64], oh[:, co2:co2 + 64],
                                    OP.subtract, [ev_oh, sg_prev])
                        rd_bank(ob, ev_oh)
                        ev_oe = rd_bank(ob, ev_ol)
                        st_reader[idx] = ev_o1
                        qt_reader[idx] = ev_o2
                        o_evac_last = [ev_oh, ev_ol]
                        emit_gateup_half(kidx, 1)
                    ev_kt = vts(ktscr[:, idx * 128:(idx + 1) * 128],
                                ktm[:, s * 512 + g * 128: s * 512 + g * 128 + 128],
                                kdecx_s[:, par * 16 + h: par * 16 + h + 1], None,
                                OP.mult, None, [ev_ktm, ev_kdecx, kt_reader[idx]])
                    d2 = list(bank_readers[ab]) + [ev_kt, ev_vtm]
                    if not prelude:
                        d2.append(ev_st)
                    bank_readers[ab] = []
                    ev_A = mm(ab, PB[ab][:, 0:128],
                              ktscr[:, idx * 128:(idx + 1) * 128],
                              vtm[:, s * 512 + g * 128: s * 512 + g * 128 + 128],
                              True, True, d2)
                    kvdeps = [ev_A, ev_kv[h], ev_blkb]
                    if not prelude:
                        kvdeps.append(ev_o2)
                    ev_kvu = rd_bank(ab, vstt(kv_s[:, h * 128:(h + 1) * 128],
                                              kv_s[:, h * 128:(h + 1) * 128],
                                              blkb_s[:, h:h + 1], PB[ab][:, 0:128],
                                              OP.mult, OP.add, kvdeps))
                    ev_kv[h] = ev_kvu
                    kt_reader[idx] = ev_A
                    attn_t_last = ev_A
                    if not prelude:
                        if kidx % SCW == SCW - 1:
                            emit_down_sc(kidx // SCW)
                        kidx += 1
            state["attn_t_last"] = attn_t_last

            if prelude:
                state["xnfm_consumed"] = attn_t_last
                return

            ev_s = []
            for oc in range(DC):
                whi, wlo, hold, bslot = get_wpair(sto_d, ho_d, oc, wmode)
                b = pr_alt[0] % 2
                pr_alt[0] += 1
                quads = [(whi[:, ec * 128:(ec + 1) * 128], wlo[:, ec * 128:(ec + 1) * 128],
                          oh[:, ec * TT:(ec + 1) * TT], ol[:, ec * TT:(ec + 1) * TT])
                         for ec in range(DC)]
                ev_mm = mm3group(b, PB[b][:, 0:TT], quads, hold + o_evac_last)
                wab_slots[bslot].append(ev_mm)
                ev_se = rd_bank(b, vcopy(qfm[:, oc * TT:(oc + 1) * TT], PB[b][:, 0:TT],
                                         [ev_mm, attn_t_last]))
                ev_s.append(ev_se)

            state["xnfm_consumed"] = xn_last_mm_box[0]

            ev_xr = []
            for s in range(NSUB):
                ev_xr.append(dma(xbuf[:, s * D:(s + 1) * D],
                                 xsrc[t0 + s * 128: t0 + s * 128 + 128, :],
                                 f"xb{s}", xbuf_free[s]))
            sg_last = None
            add_last = [None, None]
            for oc in range(DC):
                ev_g = sact(yacc[:, oc * TT:(oc + 1) * TT], yacc[:, oc * TT:(oc + 1) * TT],
                            AF.Silu, [ya_last[oc]])
                ev_sg = vtt(qfm[:, oc * TT:(oc + 1) * TT], qfm[:, oc * TT:(oc + 1) * TT],
                            yacc[:, oc * TT:(oc + 1) * TT], OP.mult, [ev_g, ev_s[oc]])
                sg_last = ev_sg
                for s in range(NSUB):
                    b = 4 + (tr_alt[0] % 2)
                    tr_alt[0] += 1
                    evt = transpose(b, PB[b][:, 0:128],
                                    qfm[:, oc * TT + s * 128: oc * TT + s * 128 + 128],
                                    [ev_sg])
                    ev_add = rd_bank(b, vtt(xbuf[:, s * D + oc * 128: s * D + oc * 128 + 128],
                                            xbuf[:, s * D + oc * 128: s * D + oc * 128 + 128],
                                            PB[b][:, 0:128], OP.add, [evt, ev_xr[s]]))
                    add_last[s] = ev_add
                    state["comb_t_last"] = evt
            state["sg_last"] = sg_last
            for s in range(NSUB):
                ev_out = gdma(y_d[t0 + s * 128: t0 + s * 128 + 128, :],
                              xbuf[:, s * D:(s + 1) * D], f"o{s}", [add_last[s]])
                xbuf_free[s] = [ev_out]

        for ti in range(NTILE):
            do_tile(ti, prelude=True)
        for ti in range(NTILE):
            do_tile(ti, prelude=False)
            if ti == 0:
                split_done[0] = [store_last[k] for k in
                                 ("swa0", "swa1", "swd0", "swd1") if k in store_last]

        P.finalize()
        block = E(nc.Block())

        @block.sync
        def _(eng):
            P.run("sync", eng, sems)

        @block.tensor
        def _(eng):
            P.run("tensor", eng, sems)

        @block.vector
        def _(eng):
            P.run("vector", eng, sems)

        @block.scalar
        def _(eng):
            P.run("scalar", eng, sems)

        @block.gpsimd
        def _(eng):
            P.run("gpsimd", eng, sems)
            eng.wait_ge(sems["D_o0"], P.dma_n["o0"] * 16)
            eng.wait_ge(sems["D_o1"], P.dma_n["o1"] * 16)

    return nc


def _tables():
    import jax
    import jax.numpy as jnp
    cpu = jax.devices("cpu")[0]
    with jax.default_device(cpu):
        i = jnp.arange(1, NH + 1, dtype=jnp.float32)
        lam = jnp.exp2(-8.0 * i / NH)
        r = jnp.arange(64, dtype=jnp.float32)
        diff = r[:, None] - r[None, :]
        diag = jnp.where(diff >= 0, jnp.exp(-lam[:, None, None] * diff), 0.0)
        q_dec = jnp.exp(-lam[:, None] * (r + 1.0))
        k_dec = jnp.exp(-lam[:, None] * (64 - 1.0 - r))
        blk = jnp.exp(-lam * 64)
        inv_freq = 1.0 / 10000.0 ** (jnp.arange(0, DH, 2, dtype=jnp.float32) / DH)
        freqs = jnp.arange(4096, dtype=jnp.float32)[:, None] * inv_freq[None, :]
        cos = jnp.concatenate([jnp.cos(freqs), jnp.cos(freqs)], axis=-1)
        sin = jnp.concatenate([jnp.sin(freqs), jnp.sin(freqs)], axis=-1)
        out = [np.asarray(t) for t in (diag, q_dec, k_dec, blk, cos, sin)]
    return out


def _retile(w, nout):
    no, din = w.shape
    assert no == nout * 128
    ndc = din // 128
    return np.ascontiguousarray(
        w.reshape(nout, 128, ndc, 128).transpose(0, 3, 2, 1)).reshape(nout, 128, din)


_NC_CACHE = {}


def kernel(L0_attn_x, L0_attn_wq, L0_attn_wk, L0_attn_wv, L0_attn_wo,
           L0_ffn_w_gate, L0_ffn_w_up, L0_ffn_w_down):
    x = np.ascontiguousarray(np.asarray(L0_attn_x, dtype=np.float32)).reshape(8 * TOKC, D)
    wq = np.asarray(L0_attn_wq, dtype=np.float32)
    wk = np.asarray(L0_attn_wk, dtype=np.float32)
    wv = np.asarray(L0_attn_wv, dtype=np.float32)
    wo = np.asarray(L0_attn_wo, dtype=np.float32)
    wg = np.asarray(L0_ffn_w_gate, dtype=np.float32)
    wu = np.asarray(L0_ffn_w_up, dtype=np.float32)
    wdn = np.asarray(L0_ffn_w_down, dtype=np.float32)

    stq = _retile(wq, NH)
    stk = _retile(wk, NKV)
    stv = _retile(wv, NKV)
    sto = _retile(wo, DC)
    stg = _retile(wg, HC)
    stu = _retile(wu, HC)
    stdw = np.ascontiguousarray(
        wdn.reshape(DC, 128, NSC, SCW, 128).transpose(2, 0, 4, 3, 1)
    ).reshape(NSC, DC, 128, SCW * 128)

    diag, q_dec, k_dec, blk, cos, sin = _tables()
    maskt = np.zeros((128, 2048), np.float32)
    for par in range(2):
        for h in range(NH):
            maskt[par * 64:(par + 1) * 64, par * 1024 + h * 64: par * 1024 + (h + 1) * 64] = diag[h].T
    qdecb = np.zeros((128, 1024), np.float32)
    for h in range(NH):
        qdecb[:, h * 64:(h + 1) * 64] = q_dec[h][None, :]
    kdecx = np.zeros((128, 32), np.float32)
    for par in range(2):
        for h in range(NH):
            kdecx[par * 64:(par + 1) * 64, par * 16 + h] = k_dec[h]
    blkb = np.broadcast_to(blk[None, :], (128, 16)).astype(np.float32).copy()
    idm = np.eye(128, dtype=np.float32)
    pmat = np.zeros((128, 128), np.float32)
    for d in range(64):
        pmat[d, d + 64] = 1.0
        pmat[d + 64, d] = -1.0
    cosp = np.ascontiguousarray(cos[0:TOKC].T)
    sinp = np.ascontiguousarray(sin[0:TOKC].T)

    if "nc" not in _NC_CACHE:
        _NC_CACHE["nc"] = build_nc()
    nc = _NC_CACHE["nc"]

    zeros_x = np.zeros((TOKC, D), np.float32)
    in_maps = []
    for c in range(8):
        half = c % 2
        b = c // 2
        xm = x[c * TOKC:(c + 1) * TOKC]
        xpm = x[b * 2 * TOKC: b * 2 * TOKC + TOKC] if half == 1 else zeros_x
        p0 = half * TOKC
        in_maps.append(dict(
            x=xm, xp=xpm, stq=stq, stk=stk, stv=stv, sto=sto, stg=stg, stu=stu,
            stdw=stdw, maskt=maskt, qdecb=qdecb, kdecx=kdecx, blkb=blkb,
            cosm=np.ascontiguousarray(cos[p0:p0 + TOKC].T),
            sinm=np.ascontiguousarray(sin[p0:p0 + TOKC].T),
            cosp=cosp, sinp=sinp, idm=idm, pmat=pmat,
            epsc=np.full((128, 1), 1e-5, np.float32),
        ))
    res = run_bass_kernel_spmd(nc, in_maps, core_ids=list(range(8)))
    y = np.concatenate([np.asarray(res.results[c]["y"]) for c in range(8)], axis=0)
    return y.reshape(4, 4096, 2048).astype(np.float32)


if __name__ == "__main__":
    nc = build_nc()
    print("built ok")


# version 05
# speedup vs baseline: 1.1067x; 1.0478x over previous
import contextlib
import numpy as np
import concourse.bass as bass
import concourse.mybir as mybir
from concourse.bass_utils import run_bass_kernel_spmd

F32 = mybir.dt.float32
AF = mybir.ActivationFunctionType
OP = mybir.AluOpType

D = 2048
HDIM = 8192
NH, NKV, GRP, DH = 16, 4, 4, 128
TOKC = 2048
TT = 256
NTILE = TOKC // TT
NSUB = TT // 128
NCHT = TT // 64
DC = D // 128
HC = HDIM // 128
NSC, SCW = 8, 8
EPS = 1e-5
INV_D = 1.0 / D

ENGS = ("sync", "tensor", "vector", "scalar", "gpsimd")


class _Ev:
    __slots__ = ("eng", "fn", "deps", "dma", "dkey", "needed", "order", "semk", "val")

    def __init__(self, eng, fn, deps, dma, dkey):
        self.eng = eng
        self.fn = fn
        self.deps = [d for d in deps if d is not None]
        self.dma = dma
        self.dkey = dkey
        self.needed = False
        self.order = -1
        self.semk = None
        self.val = None


class _Plan:
    def __init__(self):
        self.by = {e: [] for e in ENGS}
        self.dma_n = {e: 0 for e in ENGS}

    def ev(self, eng, fn, deps=(), dkey=None):
        dma = dkey is not None
        assert not (dma and eng not in ("sync", "gpsimd"))
        e = _Ev(eng, fn, deps, dma, dkey)
        e.order = len(self.by[eng])
        self.by[eng].append(e)
        return e

    def finalize(self):
        for eng in ENGS:
            for e in self.by[eng]:
                for d in e.deps:
                    if d.eng == e.eng:
                        assert d.order < e.order, f"same-engine dep out of order on {eng}"
                    else:
                        d.needed = True
        dcnt = {}
        for eng in ENGS:
            ncmp = 0
            for e in self.by[eng]:
                if e.dma:
                    dcnt[e.dkey] = dcnt.get(e.dkey, 0) + 1
                    e.semk = "D_" + e.dkey
                    e.val = dcnt[e.dkey] * 16
                elif e.needed:
                    ncmp += 1
                    e.semk = "P_" + eng
                    e.val = ncmp
        for eng in ENGS:
            for e in self.by[eng]:
                if e.dma and e.dkey == "c":
                    e.val = dcnt["c"] * 16
        self.dma_n = dcnt

    def run(self, eng, engobj, sems):
        waited = {}
        need_drain = eng in ("vector", "scalar")
        first = True
        for e in self.by[eng]:
            if need_drain and not first:
                engobj.drain()
            first = False
            for d in e.deps:
                if d.eng == eng:
                    continue
                assert d.semk is not None
                if waited.get(d.semk, 0) >= d.val:
                    continue
                engobj.wait_ge(sems[d.semk], d.val)
                waited[d.semk] = d.val
            ins = e.fn(engobj)
            if e.dma:
                ins.then_inc(sems[e.semk], 16)
            elif e.needed:
                ins.then_inc(sems[e.semk], 1)


def build_nc():
    nc = bass.Bass("TRN2", target_bir_lowering=False)
    dt = nc.dram_tensor
    x_d = dt("x", [TOKC, D], F32, kind="ExternalInput").ap()
    xp_d = dt("xp", [TOKC, D], F32, kind="ExternalInput").ap()
    stq_d = dt("stq", [NH, 128, D], F32, kind="ExternalInput").ap()
    stk_d = dt("stk", [NKV, 128, D], F32, kind="ExternalInput").ap()
    stv_d = dt("stv", [NKV, 128, D], F32, kind="ExternalInput").ap()
    sto_d = dt("sto", [DC, 128, D], F32, kind="ExternalInput").ap()
    stg_d = dt("stg", [HC, 128, D], F32, kind="ExternalInput").ap()
    stu_d = dt("stu", [HC, 128, D], F32, kind="ExternalInput").ap()
    std_d = dt("stdw", [NSC, DC, 128, SCW * 128], F32, kind="ExternalInput").ap()
    maskt_d = dt("maskt", [128, 2048], F32, kind="ExternalInput").ap()
    qdecb_d = dt("qdecb", [128, 1024], F32, kind="ExternalInput").ap()
    kdecx_d = dt("kdecx", [128, 32], F32, kind="ExternalInput").ap()
    blkb_d = dt("blkb", [128, 16], F32, kind="ExternalInput").ap()
    cosm_d = dt("cosm", [128, TOKC], F32, kind="ExternalInput").ap()
    sinm_d = dt("sinm", [128, TOKC], F32, kind="ExternalInput").ap()
    cosp_d = dt("cosp", [128, TOKC], F32, kind="ExternalInput").ap()
    sinp_d = dt("sinp", [128, TOKC], F32, kind="ExternalInput").ap()
    idm_d = dt("idm", [128, 128], F32, kind="ExternalInput").ap()
    epsc_d = dt("epsc", [128, 1], F32, kind="ExternalInput").ap()
    pmat_d = dt("pmat", [128, 128], F32, kind="ExternalInput").ap()
    y_d = dt("y", [TOKC, D], F32, kind="ExternalOutput").ap()
    BF = mybir.dt.bfloat16
    hq_d = dt("hq", [NH, 128, 2 * D], BF, kind="Internal").ap()
    hk_d = dt("hk", [NKV, 128, 2 * D], BF, kind="Internal").ap()
    hv_d = dt("hv", [NKV, 128, 2 * D], BF, kind="Internal").ap()
    ho_d = dt("ho", [DC, 128, 2 * D], BF, kind="Internal").ap()
    hg_d = dt("hg", [HC, 128, 2 * D], BF, kind="Internal").ap()
    hu_d = dt("hu", [HC, 128, 2 * D], BF, kind="Internal").ap()
    hd_d = dt("hd", [NSC, DC, 128, 2 * SCW * 128], BF, kind="Internal").ap()

    with contextlib.ExitStack() as stk_ctx:
        E = stk_ctx.enter_context
        xbuf = E(nc.sbuf_tensor("xbuf", [128, 2 * D], F32))
        xnscr = E(nc.sbuf_tensor("xnscr", [128, D], F32))
        BF = mybir.dt.bfloat16
        xnh = E(nc.sbuf_tensor("xnh", [128, DC * TT], BF))
        xnl = E(nc.sbuf_tensor("xnl", [128, DC * TT], BF))
        qfm = E(nc.sbuf_tensor("qfm", [128, NH * TT], F32))
        kfm = E(nc.sbuf_tensor("kfm", [128, NKV * TT], F32))
        vfm = E(nc.sbuf_tensor("vfm", [128, NKV * TT], F32))
        ktm = E(nc.sbuf_tensor("ktm", [128, NSUB * 512], F32))
        vtm = E(nc.sbuf_tensor("vtm", [128, NSUB * 512], F32))
        oh = E(nc.sbuf_tensor("oh", [128, NH * TT], BF))
        ol = E(nc.sbuf_tensor("ol", [128, NH * TT], BF))
        kv_s = E(nc.sbuf_tensor("kvs", [128, NH * 128], F32))
        gth = E(nc.sbuf_tensor("gth", [128, SCW * TT], BF))
        gtl = E(nc.sbuf_tensor("gtl", [128, SCW * TT], BF))
        yacc = E(nc.sbuf_tensor("yacc", [128, DC * TT], F32))
        wa = E(nc.sbuf_tensor("wa", [128, 2 * D], F32))
        wab = E(nc.sbuf_tensor("wab", [128, 2 * 2 * D], BF))
        wd = E(nc.sbuf_tensor("wdf", [128, 2 * SCW * 128], F32))
        wdb = E(nc.sbuf_tensor("wdbb", [128, 2 * 2 * SCW * 128], BF))
        ropefm = E(nc.sbuf_tensor("ropefm", [128, 2 * TT], F32))
        projscr = E(nc.sbuf_tensor("projscr", [128, 2 * TT], F32))
        ffnscr = E(nc.sbuf_tensor("ffnscr", [128, 2 * TT], F32))
        sTscr = E(nc.sbuf_tensor("sTscr", [128, 2 * 64], F32))
        qtscr = E(nc.sbuf_tensor("qtscr", [128, 2 * 64], F32))
        ktscr = E(nc.sbuf_tensor("ktscr", [128, 2 * 128], F32))
        statc = E(nc.sbuf_tensor("statc", [128, 16], F32))
        maskt_s = E(nc.sbuf_tensor("maskts", [128, 2048], F32))
        qdecb_s = E(nc.sbuf_tensor("qdecbs", [128, 1024], F32))
        kdecx_s = E(nc.sbuf_tensor("kdecxs", [128, 32], F32))
        blkb_s = E(nc.sbuf_tensor("blkbs", [128, 16], F32))
        idm_s = E(nc.sbuf_tensor("idms", [128, 128], F32))
        pmat_s = E(nc.sbuf_tensor("pmats", [128, 128], F32))
        eps_s = E(nc.sbuf_tensor("epss", [128, 1], F32))
        PB = [E(nc.psum_tensor(f"pb{i}", [128, 512], F32)) for i in range(8)]
        sems = {}
        DKEYS = ("c", "wa0", "wa1", "wd0", "wd1", "xb0", "xb1",
                 "rc", "rs", "o0", "o1", "wab0", "wab1", "wdb0", "wdb1",
                 "swa0", "swa1", "swd0", "swd1")
        for k in ("P_tensor", "P_vector", "P_scalar", "P_gpsimd") + tuple("D_" + d for d in DKEYS):
            sems[k] = E(nc.semaphore(k))

        P = _Plan()
        bank_writer = [None] * 8
        bank_readers = [[] for _ in range(8)]

        def dma(dst, src, key, deps=()):
            def fn(eng, d=dst, s=src):
                return eng.dma_start(out=d, in_=s)
            return P.ev("sync", fn, deps, dkey=key)

        def mm(b, out, l, r, start, stop, deps=()):
            def fn(eng, o=out, ll=l, rr=r, st=start, sp=stop):
                return eng.matmul(o, ll, rr, start=st, stop=sp)
            e = P.ev("tensor", fn, deps)
            bank_writer[b] = e
            return e

        def mmgroup(b, out, pairs, deps=()):
            d0 = list(deps) + bank_readers[b]
            bank_readers[b] = []
            last = None
            n = len(pairs)
            for i, (l, r) in enumerate(pairs):
                last = mm(b, out, l, r, i == 0, i == n - 1, d0 if i == 0 else ())
            return last

        def transpose(b, out, in_, deps=()):
            d0 = list(deps) + bank_readers[b]
            bank_readers[b] = []

            def fn(eng, o=out, i=in_):
                return eng.transpose(o, i, idm_s[:])
            e = P.ev("tensor", fn, d0)
            bank_writer[b] = e
            return e

        def rd_bank(b, e):
            bank_readers[b].append(e)
            return e

        def vcopy(dst, src, deps=()):
            def fn(eng, d=dst, s=src):
                return eng.tensor_copy(d, s)
            return P.ev("vector", fn, deps)

        def scopy(dst, src, deps=()):
            def fn(eng, d=dst, s=src):
                return eng.copy(d, s)
            return P.ev("scalar", fn, deps)

        def vtt(dst, a, b_, op, deps=()):
            def fn(eng, d=dst, x=a, y=b_, o=op):
                return eng.tensor_tensor(d, x, y, o)
            return P.ev("vector", fn, deps)

        def vts(dst, in0, s1, s2, op0, op1, deps=()):
            def fn(eng, d=dst, i0=in0, a=s1, b2=s2, o0=op0, o1=op1):
                if b2 is None:
                    return eng.tensor_scalar(out=d, in0=i0, scalar1=a, scalar2=None, op0=o0)
                return eng.tensor_scalar(out=d, in0=i0, scalar1=a, scalar2=b2, op0=o0, op1=o1)
            return P.ev("vector", fn, deps)

        def vstt(dst, in0, scal, in1, op0, op1, deps=()):
            def fn(eng, d=dst, i0=in0, s=scal, i1=in1, o0=op0, o1=op1):
                return eng.scalar_tensor_tensor(out=d, in0=i0, scalar=s, in1=i1, op0=o0, op1=o1)
            return P.ev("vector", fn, deps)

        def vred(dst, src, deps=()):
            def fn(eng, d=dst, s=src):
                return eng.tensor_reduce(out=d, in_=s, axis=mybir.AxisListType.X, op=OP.add)
            return P.ev("vector", fn, deps)

        def vrecip(dst, src, deps=()):
            def fn(eng, d=dst, s=src):
                return eng.reciprocal(d, s)
            return P.ev("vector", fn, deps)

        def sact(dst, src, func, deps=(), bias=None, scale=None, accum=None):
            def fn(eng, d=dst, s=src, f=func, b2=bias, sc=scale, ac=accum):
                kw = {}
                if b2 is not None:
                    kw["bias"] = b2
                if sc is not None:
                    kw["scale"] = sc
                if ac is not None:
                    kw["accum_out"] = ac
                return eng.activation(out=d, in_=s, func=f, **kw)
            return P.ev("scalar", fn, deps)

        def gdma(dst, src, key, deps=()):
            def fn(eng, d=dst, s=src):
                return eng.dma_start(out=d, in_=s)
            return P.ev("gpsimd", fn, deps, dkey=key)

        ev_maskt = dma(maskt_s[:], maskt_d, "c")
        ev_qdecb = dma(qdecb_s[:], qdecb_d, "c")
        ev_kdecx = dma(kdecx_s[:], kdecx_d, "c")
        ev_blkb = dma(blkb_s[:], blkb_d, "c")
        ev_idm = dma(idm_s[:], idm_d, "c")
        ev_pmat = dma(pmat_s[:], pmat_d, "c")
        ev_eps = dma(eps_s[:], epsc_d, "c")
        ev_kv = []
        for h in range(NH):
            def fnz(eng, h=h):
                return eng.memset(kv_s[:, h * 128:(h + 1) * 128], 0.0)
            ev_kv.append(P.ev("vector", fnz))

        wa_slots = [None, None]
        wa_i = [0]
        wab_slots = [[], []]
        wab_i = [0]
        wd_slots = [None, None]
        wd_i = [0]
        wdb_slots = [[], []]
        wdb_i = [0]
        store_last = {}
        split_done = [None]
        rope_cons = []
        xbuf_free = [[], []]
        tr_alt = [0]
        st_reader = [None, None]
        qt_reader = [None, None]
        kt_reader = [None, None]
        ffn_reader = [None, None]
        prod_reader = [None, None]
        state = {
            "xnfm_consumed": None,
            "attn_t_last": None,
            "comb_t_last": None,
            "sg_last": None,
            "xnscr_free": None,
        }

        def wa_load(src, deps=()):
            slot = wa_i[0] % 2
            wa_i[0] += 1
            ap = wa[:, slot * D:(slot + 1) * D]
            e = dma(ap, src, f"wa{slot}",
                    list(deps) + ([wa_slots[slot]] if wa_slots[slot] is not None else []))
            return slot, ap, e

        def get_wpair(srcf32, hl_d, oc, mode):
            if mode == "load":
                slot = wab_i[0] % 2
                wab_i[0] += 1
                ap = wab[:, slot * 2 * D:(slot + 1) * 2 * D]
                e = dma(ap, hl_d[oc], f"wab{slot}",
                        wab_slots[slot] + list(split_done[0] or ()))
                wab_slots[slot] = []
                hold = [e]
            else:
                slotf, wfp, ev_wf = wa_load(srcf32[oc])
                slot = wab_i[0] % 2
                wab_i[0] += 1
                ap = wab[:, slot * 2 * D:(slot + 1) * 2 * D]
                hi, lo = ap[:, 0:D], ap[:, D:2 * D]
                ev_hi = sact(hi, wfp, AF.Copy, deps=[ev_wf] + wab_slots[slot])
                ev_lo = vtt(lo, wfp, hi, OP.subtract, [ev_hi])
                wa_slots[slotf] = ev_lo
                wab_slots[slot] = []
                hold = [ev_hi, ev_lo]
                if mode == "split_store":
                    g0 = P.ev("gpsimd", (lambda eng, d=hl_d[oc][:, 0:D], s=hi:
                                         eng.dma_start(out=d, in_=s)),
                              [ev_hi, ev_lo], dkey=f"swa{slot}")
                    g1 = P.ev("gpsimd", (lambda eng, d=hl_d[oc][:, D:2 * D], s=lo:
                                         eng.dma_start(out=d, in_=s)),
                              [ev_lo], dkey=f"swa{slot}")
                    wab_slots[slot] += [g1]
                    store_last[f"swa{slot}"] = g1
            return ap[:, 0:D], ap[:, D:2 * D], hold, slot

        def wd_load(src, deps=()):
            slot = wd_i[0] % 2
            wd_i[0] += 1
            ap = wd[:, slot * 1024:(slot + 1) * 1024]
            e = dma(ap, src, f"wd{slot}",
                    list(deps) + ([wd_slots[slot]] if wd_slots[slot] is not None else []))
            return slot, ap, e

        def get_dpair(sc, oc, mode):
            W = SCW * 128
            if mode == "load":
                slot = wdb_i[0] % 2
                wdb_i[0] += 1
                ap = wdb[:, slot * 2 * W:(slot + 1) * 2 * W]
                e = dma(ap, hd_d[sc, oc], f"wdb{slot}",
                        wdb_slots[slot] + list(split_done[0] or ()))
                wdb_slots[slot] = []
                return ap[:, 0:W], ap[:, W:2 * W], [e], slot
            slotf, wfp, ev_wf = wd_load(std_d[sc, oc])
            slot = wdb_i[0] % 2
            wdb_i[0] += 1
            ap = wdb[:, slot * 2 * W:(slot + 1) * 2 * W]
            hi, lo = ap[:, 0:W], ap[:, W:2 * W]
            ev_hi = sact(hi, wfp, AF.Copy, deps=[ev_wf] + wdb_slots[slot])
            ev_lo = vtt(lo, wfp, hi, OP.subtract, [ev_hi])
            wd_slots[slotf] = ev_lo
            wdb_slots[slot] = []
            g0 = P.ev("gpsimd", (lambda eng, d=hd_d[sc, oc][:, 0:W], s=hi:
                                 eng.dma_start(out=d, in_=s)),
                      [ev_hi, ev_lo], dkey=f"swd{slot}")
            g1 = P.ev("gpsimd", (lambda eng, d=hd_d[sc, oc][:, W:2 * W], s=lo:
                                 eng.dma_start(out=d, in_=s)),
                      [ev_lo], dkey=f"swd{slot}")
            wdb_slots[slot] += [g1]
            store_last[f"swd{slot}"] = g1
            return hi, lo, [ev_hi, ev_lo], slot

        def mm3group(b, out, quads, deps=()):
            d0 = list(deps) + bank_readers[b]
            bank_readers[b] = []
            last = None
            n = len(quads)
            for i, (whi, wlo, xhi, xlo) in enumerate(quads):
                last = mm(b, out, whi, xhi, i == 0, False, d0 if i == 0 else ())
                last = mm(b, out, whi, xlo, False, False, ())
                last = mm(b, out, wlo, xhi, False, i == n - 1, ())
            return last

        def do_tile(ti, prelude):
            wmode = "split" if prelude else ("split_store" if ti == 0 else "load")
            t0 = ti * TT
            xsrc = xp_d if prelude else x_d
            cossrc = cosp_d if prelude else cosm_d
            sinsrc = sinp_d if prelude else sinm_d

            old_cons = list(rope_cons)
            rope_cons.clear()
            ev_cos = dma(ropefm[:, 0:TT], cossrc[:, t0:t0 + TT], "rc", old_cons)
            ev_sin = dma(ropefm[:, TT:2 * TT], sinsrc[:, t0:t0 + TT], "rs", old_cons)

            xn_prev = state["xnfm_consumed"]
            last_evac = None
            for s in range(NSUB):
                xb = xbuf[:, s * D:(s + 1) * D]
                ev_x = dma(xb, xsrc[t0 + s * 128: t0 + s * 128 + 128, :], f"xb{s}", xbuf_free[s])
                c0 = statc[:, s * 8 + 0: s * 8 + 1]
                c1 = statc[:, s * 8 + 1: s * 8 + 2]
                c2 = statc[:, s * 8 + 2: s * 8 + 3]
                c3 = statc[:, s * 8 + 3: s * 8 + 4]
                c4 = statc[:, s * 8 + 4: s * 8 + 5]
                c5 = statc[:, s * 8 + 5: s * 8 + 6]
                ev_red = vred(c0, xb, [ev_x])
                vts(c1, c0, INV_D, None, OP.mult, None)
                ev_musq = vtt(c3, c1, c1, OP.mult)
                ev_ssq = sact(xnscr[:], xb, AF.Square,
                              deps=[ev_x, state["xnscr_free"]], accum=c2)
                ev_var = vts(c4, c2, INV_D, None, OP.mult, None, [ev_ssq])
                ev_var2 = vtt(c4, c4, c3, OP.subtract)
                ev_std = sact(c5, c4, AF.Sqrt, deps=[ev_var2, ev_eps], bias=eps_s[:, 0:1])
                ev_rstd = vrecip(c4, c5, [ev_std])
                ev_xn = vts(xnscr[:], xb, c1, c4, OP.subtract, OP.mult, [ev_rstd])
                xbuf_free[s] = [ev_ssq, ev_xn]
                last_tr = None
                for dc in range(DC):
                    b = 4 + (tr_alt[0] % 2)
                    tr_alt[0] += 1
                    evt = transpose(b, PB[b][:, 0:128], xnscr[:, dc * 128:(dc + 1) * 128],
                                    [ev_xn, ev_idm])
                    co = dc * TT + s * 128
                    ehi = scopy(xnh[:, co:co + 128], PB[b][:, 0:128], [evt, xn_prev])
                    elo = vtt(xnl[:, co:co + 128], PB[b][:, 0:128], xnh[:, co:co + 128],
                              OP.subtract, [ehi, xn_prev])
                    rd_bank(b, ehi)
                    rd_bank(b, elo)
                    last_tr = evt
                    last_evac = [ehi, elo]
                state["xnscr_free"] = last_tr
            ev_xn_ready = last_evac

            pr_alt = [0]
            attn_prev = state["attn_t_last"]
            comb_prev = state["comb_t_last"]

            def proj(nout, wsrc, hlsrc, dstbuf, rope, tmbuf):
                finals = []
                tm_last = None
                for oc in range(nout):
                    whi, wlo, hold, bslot = get_wpair(wsrc, hlsrc, oc, wmode)
                    b = pr_alt[0] % 2
                    pr_alt[0] += 1
                    quads = [(whi[:, dc * 128:(dc + 1) * 128], wlo[:, dc * 128:(dc + 1) * 128],
                              xnh[:, dc * TT:(dc + 1) * TT], xnl[:, dc * TT:(dc + 1) * TT])
                             for dc in range(DC)]
                    ev_mm = mm3group(b, PB[b][:, 0:TT], quads, hold + ev_xn_ready)
                    wab_slots[bslot].append(ev_mm)
                    dst = dstbuf[:, oc * TT:(oc + 1) * TT]
                    ev_raw = rd_bank(b, vcopy(dst, PB[b][:, 0:TT],
                                              [ev_mm, attn_prev, comb_prev]))
                    if rope:
                        rb = 2 + (pr_alt[0] % 2)
                        d0 = list(bank_readers[rb]) + [ev_raw, ev_pmat]
                        bank_readers[rb] = []
                        ev_rot = mm(rb, PB[rb][:, 0:TT], pmat_s[:], dst, True, True, d0)
                        ev_t0 = rd_bank(rb, vtt(projscr[:, 0:TT], PB[rb][:, 0:TT],
                                                ropefm[:, TT:2 * TT], OP.mult,
                                                [ev_rot, ev_sin, ev_cos]))
                        ev_t1 = vtt(projscr[:, TT:2 * TT], dst, ropefm[:, 0:TT], OP.mult)
                        ev_rp = vtt(dst, projscr[:, TT:2 * TT], projscr[:, 0:TT], OP.add)
                        rope_cons.extend([ev_t0, ev_t1])
                        final = ev_rp
                    else:
                        final = ev_raw
                    finals.append(final)
                    if tmbuf is not None:
                        for s in range(NSUB):
                            b2 = 4 + (tr_alt[0] % 2)
                            tr_alt[0] += 1
                            evt = transpose(b2, PB[b2][:, 0:128],
                                            dstbuf[:, oc * TT + s * 128: oc * TT + s * 128 + 128],
                                            [final, attn_prev])
                            eve = scopy(tmbuf[:, s * 512 + oc * 128: s * 512 + oc * 128 + 128],
                                        PB[b2][:, 0:128], [evt, attn_prev])
                            rd_bank(b2, eve)
                            tm_last = eve
                return finals, tm_last

            ev_k, ev_ktm = proj(NKV, stk_d, hk_d, kfm, True, ktm)
            ev_v, ev_vtm = proj(NKV, stv_d, hv_d, vfm, False, vtm)
            if not prelude:
                ev_q, _ = proj(NH, stq_d, hq_d, qfm, True, None)

            ya_last = [None] * DC
            down_last_mm = [None]
            xn_last_mm_box = [None]
            gated_evs_sc = [[]]
            sg_prev = state["sg_last"]

            def emit_gateup_half(hgl, half):
                b = hgl % 2
                hcl = hgl % SCW
                if half == 0:
                    whi, wlo, hold, bslot = get_wpair(stg_d, hg_d, hgl, wmode)
                    quads = [(whi[:, dc * 128:(dc + 1) * 128], wlo[:, dc * 128:(dc + 1) * 128],
                              xnh[:, dc * TT:(dc + 1) * TT], xnl[:, dc * TT:(dc + 1) * TT])
                             for dc in range(DC)]
                    ev_gm = mm3group(b, PB[b][:, 0:TT], quads, hold + ev_xn_ready)
                    wab_slots[bslot].append(ev_gm)
                    emit_gateup_half.gm = ev_gm
                else:
                    ev_gm = emit_gateup_half.gm
                    whi, wlo, hold, bslot = get_wpair(stu_d, hu_d, hgl, wmode)
                    quads = [(whi[:, dc * 128:(dc + 1) * 128], wlo[:, dc * 128:(dc + 1) * 128],
                              xnh[:, dc * TT:(dc + 1) * TT], xnl[:, dc * TT:(dc + 1) * TT])
                             for dc in range(DC)]
                    d0 = list(hold)
                    ev_um = None
                    for i, (a1, a2, a3, a4) in enumerate(quads):
                        ev_um = mm(b, PB[b][:, TT:2 * TT], a1, a3, i == 0, False,
                                   d0 if i == 0 else ())
                        ev_um = mm(b, PB[b][:, TT:2 * TT], a1, a4, False, False, ())
                        ev_um = mm(b, PB[b][:, TT:2 * TT], a2, a3, False, i == DC - 1, ())
                    wab_slots[bslot].append(ev_um)
                    xn_last_mm_box[0] = ev_um
                    fsl = hgl % 2
                    ev_si = rd_bank(b, sact(ffnscr[:, fsl * TT:(fsl + 1) * TT],
                                            PB[b][:, 0:TT], AF.Silu,
                                            [ev_gm, ev_um, ffn_reader[fsl]]))
                    ev_pr = rd_bank(b, vtt(projscr[:, fsl * TT:(fsl + 1) * TT],
                                           ffnscr[:, fsl * TT:(fsl + 1) * TT],
                                           PB[b][:, TT:2 * TT], OP.mult,
                                           [ev_si, down_last_mm[0], prod_reader[fsl]]))
                    co = hcl * TT
                    ev_gh = sact(gth[:, co:co + TT], projscr[:, fsl * TT:(fsl + 1) * TT],
                                 AF.Copy, deps=[ev_pr, down_last_mm[0]])
                    ev_gl = P.ev("gpsimd", (lambda eng, d=gtl[:, co:co + TT],
                                            a=projscr[:, fsl * TT:(fsl + 1) * TT],
                                            b2=gth[:, co:co + TT]:
                                            eng.tensor_tensor(d, a, b2, OP.subtract)),
                                 [ev_gh, down_last_mm[0]])
                    prod_reader[fsl] = ev_gl
                    ffn_reader[fsl] = ev_pr
                    gated_evs_sc[0].extend([ev_gh, ev_gl])

            def emit_down_sc(sc):
                gevs = gated_evs_sc[0]
                last = None
                for oc in range(DC):
                    dhi, dlo, hold, bslot = get_dpair(sc, oc, wmode if wmode != "split" else "split_store")
                    yb = 4 + oc % 2
                    quads = [(dhi[:, hcl * 128:(hcl + 1) * 128], dlo[:, hcl * 128:(hcl + 1) * 128],
                              gth[:, hcl * TT:(hcl + 1) * TT], gtl[:, hcl * TT:(hcl + 1) * TT])
                             for hcl in range(SCW)]
                    ev_ym = mm3group(yb, PB[yb][:, 0:TT], quads, hold + gevs)
                    wdb_slots[bslot].append(ev_ym)
                    if sc == 0:
                        ev_ya = scopy(yacc[:, oc * TT:(oc + 1) * TT], PB[yb][:, 0:TT],
                                      [ev_ym, sg_prev])
                    else:
                        ev_ya = vtt(yacc[:, oc * TT:(oc + 1) * TT],
                                    yacc[:, oc * TT:(oc + 1) * TT], PB[yb][:, 0:TT],
                                    OP.add, [ev_ym, ya_last[oc]])
                    rd_bank(yb, ev_ya)
                    ya_last[oc] = ev_ya
                    last = ev_ym
                down_last_mm[0] = last
                gated_evs_sc[0] = []

            attn_t_last = None
            o_evac_last = None
            steps = [(c, h) for c in range(NCHT) for h in range(NH)]
            stA = [None] * len(steps)
            for k in range(len(steps) + 1):
                if k < len(steps):
                    c, h = steps[k]
                    s, par = c // 2, c % 2
                    g = h // GRP
                    idx = k % 2
                    ab = 6 + idx
                    rec = {}
                    if not prelude:
                        d0 = list(bank_readers[ab]) + [ev_k[g], ev_q[h]]
                        bank_readers[ab] = []
                        rec["sc"] = mm(ab, PB[ab][:, 128:192],
                                       kfm[:, g * TT + s * 128: g * TT + s * 128 + 128],
                                       qfm[:, h * TT + c * 64: h * TT + c * 64 + 64],
                                       True, True, d0)
                        rec["st"] = rd_bank(ab, vtt(sTscr[:, idx * 64:(idx + 1) * 64],
                                                    PB[ab][:, 128:192],
                                                    maskt_s[:, par * 1024 + h * 64: par * 1024 + h * 64 + 64],
                                                    OP.mult, [rec["sc"], ev_maskt, st_reader[idx]]))
                        rec["qt"] = vtt(qtscr[:, idx * 64:(idx + 1) * 64],
                                        qfm[:, h * TT + c * 64: h * TT + c * 64 + 64],
                                        qdecb_s[:, h * 64:(h + 1) * 64], OP.mult,
                                        [ev_q[h], ev_qdecb, qt_reader[idx]])
                    rec["kt"] = vts(ktscr[:, idx * 128:(idx + 1) * 128],
                                    ktm[:, s * 512 + g * 128: s * 512 + g * 128 + 128],
                                    kdecx_s[:, par * 16 + h: par * 16 + h + 1], None,
                                    OP.mult, None, [ev_ktm, ev_kdecx, kt_reader[idx]])
                    stA[k] = rec
                if k == 0:
                    continue
                kb = k - 1
                c, h = steps[kb]
                s, par = c // 2, c % 2
                g = h // GRP
                idx = kb % 2
                ab = 6 + idx
                rec = stA[kb]
                if not prelude:
                    emit_gateup_half(kb, 0)
                    ob = 2 + idx
                    d1 = list(bank_readers[ob]) + [rec["st"], ev_vtm]
                    bank_readers[ob] = []
                    ev_o1 = mm(ob, PB[ob][:, 0:64],
                               vtm[:, s * 512 + g * 128: s * 512 + g * 128 + 128],
                               sTscr[:, idx * 64:(idx + 1) * 64], True, False, d1)
                    ev_o2 = mm(ob, PB[ob][:, 0:64],
                               kv_s[:, h * 128:(h + 1) * 128],
                               qtscr[:, idx * 64:(idx + 1) * 64], False, True,
                               [rec["qt"], ev_kv[h]])
                    co2 = h * TT + c * 64
                    ev_oh = scopy(oh[:, co2:co2 + 64], PB[ob][:, 0:64], [ev_o2, sg_prev])
                    ev_ol = vtt(ol[:, co2:co2 + 64], PB[ob][:, 0:64], oh[:, co2:co2 + 64],
                                OP.subtract, [ev_oh, sg_prev])
                    rd_bank(ob, ev_oh)
                    rd_bank(ob, ev_ol)
                    st_reader[idx] = ev_o1
                    qt_reader[idx] = ev_o2
                    o_evac_last = [ev_oh, ev_ol]
                    emit_gateup_half(kb, 1)
                d2 = list(bank_readers[ab]) + [rec["kt"], ev_vtm]
                if not prelude:
                    d2.append(rec["st"])
                bank_readers[ab] = []
                ev_A = mm(ab, PB[ab][:, 0:128],
                          ktscr[:, idx * 128:(idx + 1) * 128],
                          vtm[:, s * 512 + g * 128: s * 512 + g * 128 + 128],
                          True, True, d2)
                kvdeps = [ev_A, ev_kv[h], ev_blkb]
                if not prelude:
                    kvdeps.append(ev_o2)
                ev_kvu = rd_bank(ab, vstt(kv_s[:, h * 128:(h + 1) * 128],
                                          kv_s[:, h * 128:(h + 1) * 128],
                                          blkb_s[:, h:h + 1], PB[ab][:, 0:128],
                                          OP.mult, OP.add, kvdeps))
                ev_kv[h] = ev_kvu
                kt_reader[idx] = ev_A
                attn_t_last = ev_A
                if not prelude and kb % SCW == SCW - 1:
                    emit_down_sc(kb // SCW)
            state["attn_t_last"] = attn_t_last

            if prelude:
                state["xnfm_consumed"] = attn_t_last
                return

            ev_s = []
            for oc in range(DC):
                whi, wlo, hold, bslot = get_wpair(sto_d, ho_d, oc, wmode)
                b = pr_alt[0] % 2
                pr_alt[0] += 1
                quads = [(whi[:, ec * 128:(ec + 1) * 128], wlo[:, ec * 128:(ec + 1) * 128],
                          oh[:, ec * TT:(ec + 1) * TT], ol[:, ec * TT:(ec + 1) * TT])
                         for ec in range(DC)]
                ev_mm = mm3group(b, PB[b][:, 0:TT], quads, hold + o_evac_last)
                wab_slots[bslot].append(ev_mm)
                ev_se = rd_bank(b, vcopy(qfm[:, oc * TT:(oc + 1) * TT], PB[b][:, 0:TT],
                                         [ev_mm, attn_t_last]))
                ev_s.append(ev_se)

            state["xnfm_consumed"] = xn_last_mm_box[0]

            ev_xr = []
            for s in range(NSUB):
                ev_xr.append(dma(xbuf[:, s * D:(s + 1) * D],
                                 xsrc[t0 + s * 128: t0 + s * 128 + 128, :],
                                 f"xb{s}", xbuf_free[s]))
            sg_last = None
            add_last = [None, None]
            for oc in range(DC):
                ev_g = sact(yacc[:, oc * TT:(oc + 1) * TT], yacc[:, oc * TT:(oc + 1) * TT],
                            AF.Silu, [ya_last[oc]])
                ev_sg = vtt(qfm[:, oc * TT:(oc + 1) * TT], qfm[:, oc * TT:(oc + 1) * TT],
                            yacc[:, oc * TT:(oc + 1) * TT], OP.mult, [ev_g, ev_s[oc]])
                sg_last = ev_sg
                for s in range(NSUB):
                    b = 4 + (tr_alt[0] % 2)
                    tr_alt[0] += 1
                    evt = transpose(b, PB[b][:, 0:128],
                                    qfm[:, oc * TT + s * 128: oc * TT + s * 128 + 128],
                                    [ev_sg])
                    ev_add = rd_bank(b, vtt(xbuf[:, s * D + oc * 128: s * D + oc * 128 + 128],
                                            xbuf[:, s * D + oc * 128: s * D + oc * 128 + 128],
                                            PB[b][:, 0:128], OP.add, [evt, ev_xr[s]]))
                    add_last[s] = ev_add
                    state["comb_t_last"] = evt
            state["sg_last"] = sg_last
            for s in range(NSUB):
                ev_out = gdma(y_d[t0 + s * 128: t0 + s * 128 + 128, :],
                              xbuf[:, s * D:(s + 1) * D], f"o{s}", [add_last[s]])
                xbuf_free[s] = [ev_out]

        for ti in range(NTILE):
            do_tile(ti, prelude=True)
        for ti in range(NTILE):
            do_tile(ti, prelude=False)
            if ti == 0:
                split_done[0] = [store_last[k] for k in
                                 ("swa0", "swa1", "swd0", "swd1") if k in store_last]

        P.finalize()
        block = E(nc.Block())

        @block.sync
        def _(eng):
            P.run("sync", eng, sems)

        @block.tensor
        def _(eng):
            P.run("tensor", eng, sems)

        @block.vector
        def _(eng):
            P.run("vector", eng, sems)

        @block.scalar
        def _(eng):
            P.run("scalar", eng, sems)

        @block.gpsimd
        def _(eng):
            P.run("gpsimd", eng, sems)
            eng.wait_ge(sems["D_o0"], P.dma_n["o0"] * 16)
            eng.wait_ge(sems["D_o1"], P.dma_n["o1"] * 16)

    return nc


def _tables():
    import jax
    import jax.numpy as jnp
    cpu = jax.devices("cpu")[0]
    with jax.default_device(cpu):
        i = jnp.arange(1, NH + 1, dtype=jnp.float32)
        lam = jnp.exp2(-8.0 * i / NH)
        r = jnp.arange(64, dtype=jnp.float32)
        diff = r[:, None] - r[None, :]
        diag = jnp.where(diff >= 0, jnp.exp(-lam[:, None, None] * diff), 0.0)
        q_dec = jnp.exp(-lam[:, None] * (r + 1.0))
        k_dec = jnp.exp(-lam[:, None] * (64 - 1.0 - r))
        blk = jnp.exp(-lam * 64)
        inv_freq = 1.0 / 10000.0 ** (jnp.arange(0, DH, 2, dtype=jnp.float32) / DH)
        freqs = jnp.arange(4096, dtype=jnp.float32)[:, None] * inv_freq[None, :]
        cos = jnp.concatenate([jnp.cos(freqs), jnp.cos(freqs)], axis=-1)
        sin = jnp.concatenate([jnp.sin(freqs), jnp.sin(freqs)], axis=-1)
        out = [np.asarray(t) for t in (diag, q_dec, k_dec, blk, cos, sin)]
    return out


def _retile(w, nout):
    no, din = w.shape
    assert no == nout * 128
    ndc = din // 128
    return np.ascontiguousarray(
        w.reshape(nout, 128, ndc, 128).transpose(0, 3, 2, 1)).reshape(nout, 128, din)


_NC_CACHE = {}


def kernel(L0_attn_x, L0_attn_wq, L0_attn_wk, L0_attn_wv, L0_attn_wo,
           L0_ffn_w_gate, L0_ffn_w_up, L0_ffn_w_down):
    x = np.ascontiguousarray(np.asarray(L0_attn_x, dtype=np.float32)).reshape(8 * TOKC, D)
    wq = np.asarray(L0_attn_wq, dtype=np.float32)
    wk = np.asarray(L0_attn_wk, dtype=np.float32)
    wv = np.asarray(L0_attn_wv, dtype=np.float32)
    wo = np.asarray(L0_attn_wo, dtype=np.float32)
    wg = np.asarray(L0_ffn_w_gate, dtype=np.float32)
    wu = np.asarray(L0_ffn_w_up, dtype=np.float32)
    wdn = np.asarray(L0_ffn_w_down, dtype=np.float32)

    stq = _retile(wq, NH)
    stk = _retile(wk, NKV)
    stv = _retile(wv, NKV)
    sto = _retile(wo, DC)
    stg = _retile(wg, HC)
    stu = _retile(wu, HC)
    stdw = np.ascontiguousarray(
        wdn.reshape(DC, 128, NSC, SCW, 128).transpose(2, 0, 4, 3, 1)
    ).reshape(NSC, DC, 128, SCW * 128)

    diag, q_dec, k_dec, blk, cos, sin = _tables()
    maskt = np.zeros((128, 2048), np.float32)
    for par in range(2):
        for h in range(NH):
            maskt[par * 64:(par + 1) * 64, par * 1024 + h * 64: par * 1024 + (h + 1) * 64] = diag[h].T
    qdecb = np.zeros((128, 1024), np.float32)
    for h in range(NH):
        qdecb[:, h * 64:(h + 1) * 64] = q_dec[h][None, :]
    kdecx = np.zeros((128, 32), np.float32)
    for par in range(2):
        for h in range(NH):
            kdecx[par * 64:(par + 1) * 64, par * 16 + h] = k_dec[h]
    blkb = np.broadcast_to(blk[None, :], (128, 16)).astype(np.float32).copy()
    idm = np.eye(128, dtype=np.float32)
    pmat = np.zeros((128, 128), np.float32)
    for d in range(64):
        pmat[d, d + 64] = 1.0
        pmat[d + 64, d] = -1.0
    cosp = np.ascontiguousarray(cos[0:TOKC].T)
    sinp = np.ascontiguousarray(sin[0:TOKC].T)

    if "nc" not in _NC_CACHE:
        _NC_CACHE["nc"] = build_nc()
    nc = _NC_CACHE["nc"]

    zeros_x = np.zeros((TOKC, D), np.float32)
    in_maps = []
    for c in range(8):
        half = c % 2
        b = c // 2
        xm = x[c * TOKC:(c + 1) * TOKC]
        xpm = x[b * 2 * TOKC: b * 2 * TOKC + TOKC] if half == 1 else zeros_x
        p0 = half * TOKC
        in_maps.append(dict(
            x=xm, xp=xpm, stq=stq, stk=stk, stv=stv, sto=sto, stg=stg, stu=stu,
            stdw=stdw, maskt=maskt, qdecb=qdecb, kdecx=kdecx, blkb=blkb,
            cosm=np.ascontiguousarray(cos[p0:p0 + TOKC].T),
            sinm=np.ascontiguousarray(sin[p0:p0 + TOKC].T),
            cosp=cosp, sinp=sinp, idm=idm, pmat=pmat,
            epsc=np.full((128, 1), 1e-5, np.float32),
        ))
    res = run_bass_kernel_spmd(nc, in_maps, core_ids=list(range(8)))
    y = np.concatenate([np.asarray(res.results[c]["y"]) for c in range(8)], axis=0)
    return y.reshape(4, 4096, 2048).astype(np.float32)


if __name__ == "__main__":
    nc = build_nc()
    print("built ok")


# version 06 (final)
# speedup vs baseline: 1.1973x; 1.0818x over previous
import contextlib
import numpy as np
import concourse.bass as bass
import concourse.mybir as mybir
from concourse.bass_utils import run_bass_kernel_spmd

F32 = mybir.dt.float32
AF = mybir.ActivationFunctionType
OP = mybir.AluOpType

D = 2048
HDIM = 8192
NH, NKV, GRP, DH = 16, 4, 4, 128
TOKC = 2048
TT = 256
NTILE = TOKC // TT
NSUB = TT // 128
NCHT = TT // 64
DC = D // 128
HC = HDIM // 128
NSC, SCW = 8, 8
EPS = 1e-5
INV_D = 1.0 / D

ENGS = ("sync", "tensor", "vector", "scalar", "gpsimd")


class _Ev:
    __slots__ = ("eng", "fn", "deps", "dma", "dkey", "needed", "order", "semk", "val",
                 "rd", "wr", "unk")

    def __init__(self, eng, fn, deps, dma, dkey, rd=(), wr=(), unk=False):
        self.eng = eng
        self.fn = fn
        self.deps = [d for d in deps if d is not None]
        self.dma = dma
        self.dkey = dkey
        self.needed = False
        self.order = -1
        self.semk = None
        self.val = None
        self.rd = [r for r in (_rng(a) for a in rd) if r is not None]
        self.wr = [r for r in (_rng(a) for a in wr) if r is not None]
        self.unk = unk


def _rng(ap):
    if ap is None or not hasattr(ap, "name"):
        return None
    p0 = ap.base_partition()
    p1 = p0 + ap.partition_size()
    e0 = ap.offset
    e1 = e0 + ap.free_size()
    return (ap.name, p0, p1, e0, e1)


def _ovl(a, b):
    return (a[0] == b[0] and a[1] < b[2] and b[1] < a[2]
            and a[3] < b[4] and b[3] < a[4])


class _Plan:
    def __init__(self):
        self.by = {e: [] for e in ENGS}
        self.dma_n = {e: 0 for e in ENGS}

    def ev(self, eng, fn, deps=(), dkey=None, rd=(), wr=(), unk=False):
        dma = dkey is not None
        assert not (dma and eng not in ("sync", "gpsimd"))
        e = _Ev(eng, fn, deps, dma, dkey, rd=rd, wr=wr, unk=unk)
        e.order = len(self.by[eng])
        self.by[eng].append(e)
        return e

    def finalize(self):
        for eng in ENGS:
            for e in self.by[eng]:
                for d in e.deps:
                    if d.eng == e.eng:
                        assert d.order < e.order, f"same-engine dep out of order on {eng}"
                    else:
                        d.needed = True
        dcnt = {}
        for eng in ENGS:
            ncmp = 0
            for e in self.by[eng]:
                if e.dma:
                    dcnt[e.dkey] = dcnt.get(e.dkey, 0) + 1
                    e.semk = "D_" + e.dkey
                    e.val = dcnt[e.dkey] * 16
                elif e.needed:
                    ncmp += 1
                    e.semk = "P_" + eng
                    e.val = ncmp
        for eng in ENGS:
            for e in self.by[eng]:
                if e.dma and e.dkey == "c":
                    e.val = dcnt["c"] * 16
        self.dma_n = dcnt

    def run(self, eng, engobj, sems):
        waited = {}
        hazard_track = eng in ("vector", "scalar")
        wsince = {}
        rsince = {}
        for e in self.by[eng]:
            if hazard_track:
                need = e.unk
                if not need:
                    for r in e.rd:
                        for w in wsince.get(r[0], ()):
                            if _ovl(r, w):
                                need = True
                                break
                        if need:
                            break
                if not need:
                    for w in e.wr:
                        for p in wsince.get(w[0], ()):
                            if _ovl(w, p):
                                need = True
                                break
                        if not need:
                            for p in rsince.get(w[0], ()):
                                if _ovl(w, p):
                                    need = True
                                    break
                        if need:
                            break
                if need:
                    engobj.drain()
                    wsince = {}
                    rsince = {}
                for r in e.rd:
                    rsince.setdefault(r[0], []).append(r)
                for w in e.wr:
                    wsince.setdefault(w[0], []).append(w)
            for d in e.deps:
                if d.eng == eng and not d.dma:
                    continue
                assert d.semk is not None
                if waited.get(d.semk, 0) >= d.val:
                    continue
                engobj.wait_ge(sems[d.semk], d.val)
                waited[d.semk] = d.val
            ins = e.fn(engobj)
            if e.dma:
                ins.then_inc(sems[e.semk], 16)
            elif e.needed:
                ins.then_inc(sems[e.semk], 1)


def build_nc():
    nc = bass.Bass("TRN2", target_bir_lowering=False)
    dt = nc.dram_tensor
    x_d = dt("x", [TOKC, D], F32, kind="ExternalInput").ap()
    xp_d = dt("xp", [TOKC, D], F32, kind="ExternalInput").ap()
    stq_d = dt("stq", [NH, 128, D], F32, kind="ExternalInput").ap()
    stk_d = dt("stk", [NKV, 128, D], F32, kind="ExternalInput").ap()
    stv_d = dt("stv", [NKV, 128, D], F32, kind="ExternalInput").ap()
    sto_d = dt("sto", [DC, 128, D], F32, kind="ExternalInput").ap()
    stg_d = dt("stg", [HC, 128, D], F32, kind="ExternalInput").ap()
    stu_d = dt("stu", [HC, 128, D], F32, kind="ExternalInput").ap()
    std_d = dt("stdw", [NSC, DC, 128, SCW * 128], F32, kind="ExternalInput").ap()
    maskt_d = dt("maskt", [128, 2048], F32, kind="ExternalInput").ap()
    qdecb_d = dt("qdecb", [128, 1024], F32, kind="ExternalInput").ap()
    kdecx_d = dt("kdecx", [128, 32], F32, kind="ExternalInput").ap()
    blkb_d = dt("blkb", [128, 16], F32, kind="ExternalInput").ap()
    cosm_d = dt("cosm", [128, TOKC], F32, kind="ExternalInput").ap()
    sinm_d = dt("sinm", [128, TOKC], F32, kind="ExternalInput").ap()
    cosp_d = dt("cosp", [128, TOKC], F32, kind="ExternalInput").ap()
    sinp_d = dt("sinp", [128, TOKC], F32, kind="ExternalInput").ap()
    idm_d = dt("idm", [128, 128], F32, kind="ExternalInput").ap()
    epsc_d = dt("epsc", [128, 1], F32, kind="ExternalInput").ap()
    pmat_d = dt("pmat", [128, 128], F32, kind="ExternalInput").ap()
    y_d = dt("y", [TOKC, D], F32, kind="ExternalOutput").ap()
    BF = mybir.dt.bfloat16
    hq_d = dt("hq", [NH, 128, 2 * D], BF, kind="Internal").ap()
    hk_d = dt("hk", [NKV, 128, 2 * D], BF, kind="Internal").ap()
    hv_d = dt("hv", [NKV, 128, 2 * D], BF, kind="Internal").ap()
    ho_d = dt("ho", [DC, 128, 2 * D], BF, kind="Internal").ap()
    hg_d = dt("hg", [HC, 128, 2 * D], BF, kind="Internal").ap()
    hu_d = dt("hu", [HC, 128, 2 * D], BF, kind="Internal").ap()
    hd_d = dt("hd", [NSC, DC, 128, 2 * SCW * 128], BF, kind="Internal").ap()

    with contextlib.ExitStack() as stk_ctx:
        E = stk_ctx.enter_context
        xbuf = E(nc.sbuf_tensor("xbuf", [128, 2 * D], F32))
        xnscr = E(nc.sbuf_tensor("xnscr", [128, D], F32))
        BF = mybir.dt.bfloat16
        xnh = E(nc.sbuf_tensor("xnh", [128, DC * TT], BF))
        xnl = E(nc.sbuf_tensor("xnl", [128, DC * TT], BF))
        qfm = E(nc.sbuf_tensor("qfm", [128, NH * TT], F32))
        kfm = E(nc.sbuf_tensor("kfm", [128, NKV * TT], F32))
        vfm = E(nc.sbuf_tensor("vfm", [128, NKV * TT], F32))
        ktm = E(nc.sbuf_tensor("ktm", [128, NSUB * 512], F32))
        vtm = E(nc.sbuf_tensor("vtm", [128, NSUB * 512], F32))
        oh = E(nc.sbuf_tensor("oh", [128, NH * TT], BF))
        ol = E(nc.sbuf_tensor("ol", [128, NH * TT], BF))
        kv_s = E(nc.sbuf_tensor("kvs", [128, NH * 128], F32))
        gth = E(nc.sbuf_tensor("gth", [128, SCW * TT], BF))
        gtl = E(nc.sbuf_tensor("gtl", [128, SCW * TT], BF))
        yacc = E(nc.sbuf_tensor("yacc", [128, DC * TT], F32))
        wa = E(nc.sbuf_tensor("wa", [128, 2 * D], F32))
        wab = E(nc.sbuf_tensor("wab", [128, 2 * 2 * D], BF))
        wd = E(nc.sbuf_tensor("wdf", [128, 2 * SCW * 128], F32))
        wdb = E(nc.sbuf_tensor("wdbb", [128, 2 * 2 * SCW * 128], BF))
        ropefm = E(nc.sbuf_tensor("ropefm", [128, 2 * TT], F32))
        projscr = E(nc.sbuf_tensor("projscr", [128, 2 * TT], F32))
        ffnscr = E(nc.sbuf_tensor("ffnscr", [128, 2 * TT], F32))
        sTscr = E(nc.sbuf_tensor("sTscr", [128, 2 * 64], F32))
        qtscr = E(nc.sbuf_tensor("qtscr", [128, 2 * 64], F32))
        ktscr = E(nc.sbuf_tensor("ktscr", [128, 2 * 128], F32))
        statc = E(nc.sbuf_tensor("statc", [128, 16], F32))
        maskt_s = E(nc.sbuf_tensor("maskts", [128, 2048], F32))
        qdecb_s = E(nc.sbuf_tensor("qdecbs", [128, 1024], F32))
        kdecx_s = E(nc.sbuf_tensor("kdecxs", [128, 32], F32))
        blkb_s = E(nc.sbuf_tensor("blkbs", [128, 16], F32))
        idm_s = E(nc.sbuf_tensor("idms", [128, 128], F32))
        pmat_s = E(nc.sbuf_tensor("pmats", [128, 128], F32))
        eps_s = E(nc.sbuf_tensor("epss", [128, 1], F32))
        PB = [E(nc.psum_tensor(f"pb{i}", [128, 512], F32)) for i in range(8)]
        sems = {}
        DKEYS = ("c", "wa0", "wa1", "wd0", "wd1", "xb0", "xb1",
                 "rc", "rs", "o0", "o1", "wab0", "wab1", "wdb0", "wdb1",
                 "swa0", "swa1", "swd0", "swd1")
        for k in ("P_tensor", "P_vector", "P_scalar", "P_gpsimd") + tuple("D_" + d for d in DKEYS):
            sems[k] = E(nc.semaphore(k))

        P = _Plan()
        bank_writer = [None] * 8
        bank_readers = [[] for _ in range(8)]

        def dma(dst, src, key, deps=()):
            def fn(eng, d=dst, s=src):
                return eng.dma_start(out=d, in_=s)
            return P.ev("sync", fn, deps, dkey=key)

        def mm(b, out, l, r, start, stop, deps=()):
            def fn(eng, o=out, ll=l, rr=r, st=start, sp=stop):
                return eng.matmul(o, ll, rr, start=st, stop=sp)
            e = P.ev("tensor", fn, deps)
            bank_writer[b] = e
            return e

        def mmgroup(b, out, pairs, deps=()):
            d0 = list(deps) + bank_readers[b]
            bank_readers[b] = []
            last = None
            n = len(pairs)
            for i, (l, r) in enumerate(pairs):
                last = mm(b, out, l, r, i == 0, i == n - 1, d0 if i == 0 else ())
            return last

        def transpose(b, out, in_, deps=()):
            d0 = list(deps) + bank_readers[b]
            bank_readers[b] = []

            def fn(eng, o=out, i=in_):
                return eng.transpose(o, i, idm_s[:])
            e = P.ev("tensor", fn, d0)
            bank_writer[b] = e
            return e

        def rd_bank(b, e):
            bank_readers[b].append(e)
            return e

        def vcopy(dst, src, deps=()):
            def fn(eng, d=dst, s=src):
                return eng.tensor_copy(d, s)
            return P.ev("vector", fn, deps, rd=[src], wr=[dst])

        def scopy(dst, src, deps=()):
            def fn(eng, d=dst, s=src):
                return eng.copy(d, s)
            return P.ev("scalar", fn, deps, rd=[src], wr=[dst])

        def vtt(dst, a, b_, op, deps=()):
            def fn(eng, d=dst, x=a, y=b_, o=op):
                return eng.tensor_tensor(d, x, y, o)
            return P.ev("vector", fn, deps, rd=[a, b_], wr=[dst])

        def vts(dst, in0, s1, s2, op0, op1, deps=()):
            def fn(eng, d=dst, i0=in0, a=s1, b2=s2, o0=op0, o1=op1):
                if b2 is None:
                    return eng.tensor_scalar(out=d, in0=i0, scalar1=a, scalar2=None, op0=o0)
                return eng.tensor_scalar(out=d, in0=i0, scalar1=a, scalar2=b2, op0=o0, op1=o1)
            return P.ev("vector", fn, deps, rd=[in0, s1, s2], wr=[dst])

        def vstt(dst, in0, scal, in1, op0, op1, deps=()):
            def fn(eng, d=dst, i0=in0, s=scal, i1=in1, o0=op0, o1=op1):
                return eng.scalar_tensor_tensor(out=d, in0=i0, scalar=s, in1=i1, op0=o0, op1=o1)
            return P.ev("vector", fn, deps, rd=[in0, scal, in1], wr=[dst])

        def vred(dst, src, deps=()):
            def fn(eng, d=dst, s=src):
                return eng.tensor_reduce(out=d, in_=s, axis=mybir.AxisListType.X, op=OP.add)
            return P.ev("vector", fn, deps, rd=[src], wr=[dst])

        def vrecip(dst, src, deps=()):
            def fn(eng, d=dst, s=src):
                return eng.reciprocal(d, s)
            return P.ev("vector", fn, deps, rd=[src], wr=[dst])

        def sact(dst, src, func, deps=(), bias=None, scale=None, accum=None):
            def fn(eng, d=dst, s=src, f=func, b2=bias, sc=scale, ac=accum):
                kw = {}
                if b2 is not None:
                    kw["bias"] = b2
                if sc is not None:
                    kw["scale"] = sc
                if ac is not None:
                    kw["accum_out"] = ac
                return eng.activation(out=d, in_=s, func=f, **kw)
            return P.ev("scalar", fn, deps, rd=[src, bias, scale], wr=[dst, accum])

        def gdma(dst, src, key, deps=()):
            def fn(eng, d=dst, s=src):
                return eng.dma_start(out=d, in_=s)
            return P.ev("gpsimd", fn, deps, dkey=key)

        ev_maskt = dma(maskt_s[:], maskt_d, "c")
        ev_qdecb = dma(qdecb_s[:], qdecb_d, "c")
        ev_kdecx = dma(kdecx_s[:], kdecx_d, "c")
        ev_blkb = dma(blkb_s[:], blkb_d, "c")
        ev_idm = dma(idm_s[:], idm_d, "c")
        ev_pmat = dma(pmat_s[:], pmat_d, "c")
        ev_eps = dma(eps_s[:], epsc_d, "c")
        ev_kv = []
        for h in range(NH):
            def fnz(eng, h=h):
                return eng.memset(kv_s[:, h * 128:(h + 1) * 128], 0.0)
            ev_kv.append(P.ev("vector", fnz, wr=[kv_s[:, h * 128:(h + 1) * 128]]))

        wa_slots = [None, None]
        wa_i = [0]
        wab_slots = [[], []]
        wab_i = [0]
        wd_slots = [None, None]
        wd_i = [0]
        wdb_slots = [[], []]
        wdb_i = [0]
        store_last = {}
        kv_bar = [None]
        rest_bar = [None]
        rope_cons = []
        xbuf_free = [[], []]
        tr_alt = [0]
        st_reader = [None, None]
        qt_reader = [None, None]
        kt_reader = [None, None]
        ffn_reader = [None, None]
        prod_reader = [None, None]
        state = {
            "xnfm_consumed": None,
            "attn_t_last": None,
            "comb_t_last": None,
            "sg_last": None,
            "xnscr_free": None,
        }

        def wa_load(src, deps=()):
            slot = wa_i[0] % 2
            wa_i[0] += 1
            ap = wa[:, slot * D:(slot + 1) * D]
            e = dma(ap, src, f"wa{slot}",
                    list(deps) + ([wa_slots[slot]] if wa_slots[slot] is not None else []))
            return slot, ap, e

        def get_wpair(srcf32, hl_d, oc, mode, barrier):
            if mode == "load":
                slot = wab_i[0] % 2
                wab_i[0] += 1
                ap = wab[:, slot * 2 * D:(slot + 1) * 2 * D]
                ldf = dma if slot == 0 else gdma
                e = ldf(ap, hl_d[oc], f"wab{slot}",
                        wab_slots[slot] + list(barrier or ()))
                wab_slots[slot] = []
                hold = [e]
            else:
                slotf, wfp, ev_wf = wa_load(srcf32[oc])
                slot = wab_i[0] % 2
                wab_i[0] += 1
                ap = wab[:, slot * 2 * D:(slot + 1) * 2 * D]
                hi, lo = ap[:, 0:D], ap[:, D:2 * D]
                ev_hi = sact(hi, wfp, AF.Copy, deps=[ev_wf] + wab_slots[slot])
                ev_lo = vtt(lo, wfp, hi, OP.subtract, [ev_hi])
                wa_slots[slotf] = ev_lo
                wab_slots[slot] = []
                hold = [ev_hi, ev_lo]
                g0 = P.ev("gpsimd", (lambda eng, d=hl_d[oc][:, 0:D], s=hi:
                                     eng.dma_start(out=d, in_=s)),
                          [ev_hi, ev_lo], dkey=f"swa{slot}")
                g1 = P.ev("gpsimd", (lambda eng, d=hl_d[oc][:, D:2 * D], s=lo:
                                     eng.dma_start(out=d, in_=s)),
                          [ev_lo], dkey=f"swa{slot}")
                wab_slots[slot] += [g1]
                store_last[f"swa{slot}"] = g1
            return ap[:, 0:D], ap[:, D:2 * D], hold, slot

        def wd_load(src, deps=()):
            slot = wd_i[0] % 2
            wd_i[0] += 1
            ap = wd[:, slot * 1024:(slot + 1) * 1024]
            e = dma(ap, src, f"wd{slot}",
                    list(deps) + ([wd_slots[slot]] if wd_slots[slot] is not None else []))
            return slot, ap, e

        def get_dpair(sc, oc, mode, barrier):
            W = SCW * 128
            if mode == "load":
                slot = wdb_i[0] % 2
                wdb_i[0] += 1
                ap = wdb[:, slot * 2 * W:(slot + 1) * 2 * W]
                ldf = dma if slot == 0 else gdma
                e = ldf(ap, hd_d[sc, oc], f"wdb{slot}",
                        wdb_slots[slot] + list(barrier or ()))
                wdb_slots[slot] = []
                return ap[:, 0:W], ap[:, W:2 * W], [e], slot
            slotf, wfp, ev_wf = wd_load(std_d[sc, oc])
            slot = wdb_i[0] % 2
            wdb_i[0] += 1
            ap = wdb[:, slot * 2 * W:(slot + 1) * 2 * W]
            hi, lo = ap[:, 0:W], ap[:, W:2 * W]
            ev_hi = sact(hi, wfp, AF.Copy, deps=[ev_wf] + wdb_slots[slot])
            ev_lo = vtt(lo, wfp, hi, OP.subtract, [ev_hi])
            wd_slots[slotf] = ev_lo
            wdb_slots[slot] = []
            g0 = P.ev("gpsimd", (lambda eng, d=hd_d[sc, oc][:, 0:W], s=hi:
                                 eng.dma_start(out=d, in_=s)),
                      [ev_hi, ev_lo], dkey=f"swd{slot}")
            g1 = P.ev("gpsimd", (lambda eng, d=hd_d[sc, oc][:, W:2 * W], s=lo:
                                 eng.dma_start(out=d, in_=s)),
                      [ev_lo], dkey=f"swd{slot}")
            wdb_slots[slot] += [g1]
            store_last[f"swd{slot}"] = g1
            return hi, lo, [ev_hi, ev_lo], slot

        def mm3group(b, out, quads, deps=(), qdeps=None):
            d0 = list(deps) + bank_readers[b]
            bank_readers[b] = []
            last = None
            n = len(quads)
            for i, (whi, wlo, xhi, xlo) in enumerate(quads):
                di = list(d0) if i == 0 else []
                if qdeps is not None:
                    di += qdeps[i]
                last = mm(b, out, whi, xhi, i == 0, False, di)
                last = mm(b, out, whi, xlo, False, False, ())
                last = mm(b, out, wlo, xhi, False, i == n - 1, ())
            return last

        def do_tile(ti, prelude):
            if prelude:
                wm_kv = "split_store" if ti == 0 else "load"
                wm_rest = None
            else:
                wm_kv = "load"
                wm_rest = "split_store" if ti == 0 else "load"
            t0 = ti * TT
            xsrc = xp_d if prelude else x_d
            cossrc = cosp_d if prelude else cosm_d
            sinsrc = sinp_d if prelude else sinm_d

            old_cons = list(rope_cons)
            rope_cons.clear()
            ev_cos = dma(ropefm[:, 0:TT], cossrc[:, t0:t0 + TT], "rc", old_cons)
            ev_sin = dma(ropefm[:, TT:2 * TT], sinsrc[:, t0:t0 + TT], "rs", old_cons)

            xn_prev = state["xnfm_consumed"]
            last_evac = None
            xn_evac = [[None] * DC for _ in range(NSUB)]
            for s in range(NSUB):
                xb = xbuf[:, s * D:(s + 1) * D]
                ev_x = dma(xb, xsrc[t0 + s * 128: t0 + s * 128 + 128, :], f"xb{s}", xbuf_free[s])
                c0 = statc[:, s * 8 + 0: s * 8 + 1]
                c1 = statc[:, s * 8 + 1: s * 8 + 2]
                c2 = statc[:, s * 8 + 2: s * 8 + 3]
                c3 = statc[:, s * 8 + 3: s * 8 + 4]
                c4 = statc[:, s * 8 + 4: s * 8 + 5]
                c5 = statc[:, s * 8 + 5: s * 8 + 6]
                ev_red = vred(c0, xb, [ev_x])
                vts(c1, c0, INV_D, None, OP.mult, None)
                ev_musq = vtt(c3, c1, c1, OP.mult)
                ev_ssq = sact(xnscr[:], xb, AF.Square,
                              deps=[ev_x, state["xnscr_free"]], accum=c2)
                ev_var = vts(c4, c2, INV_D, None, OP.mult, None, [ev_ssq])
                ev_var2 = vtt(c4, c4, c3, OP.subtract)
                ev_std = sact(c5, c4, AF.Sqrt, deps=[ev_var2, ev_eps], bias=eps_s[:, 0:1])
                ev_rstd = vrecip(c4, c5, [ev_std])
                ev_xn = vts(xnscr[:], xb, c1, c4, OP.subtract, OP.mult, [ev_rstd])
                xbuf_free[s] = [ev_ssq, ev_xn]
                last_tr = None
                for dc in range(DC):
                    b = 4 + (tr_alt[0] % 2)
                    tr_alt[0] += 1
                    evt = transpose(b, PB[b][:, 0:128], xnscr[:, dc * 128:(dc + 1) * 128],
                                    [ev_xn, ev_idm])
                    co = dc * TT + s * 128
                    ehi = scopy(xnh[:, co:co + 128], PB[b][:, 0:128], [evt, xn_prev])
                    elo = vtt(xnl[:, co:co + 128], PB[b][:, 0:128], xnh[:, co:co + 128],
                              OP.subtract, [ehi, xn_prev])
                    rd_bank(b, ehi)
                    rd_bank(b, elo)
                    last_tr = evt
                    last_evac = [ehi, elo]
                    xn_evac[s][dc] = (ehi, elo)
                state["xnscr_free"] = last_tr
            xn_qd = [[xn_evac[1][dc][0], xn_evac[1][dc][1]] for dc in range(DC)]

            pr_alt = [0]
            attn_prev = state["attn_t_last"]
            comb_prev = state["comb_t_last"]

            def proj(nout, wsrc, hlsrc, dstbuf, rope, tmbuf, wm, bar):
                finals = []
                tm_last = None
                for oc in range(nout):
                    whi, wlo, hold, bslot = get_wpair(wsrc, hlsrc, oc, wm, bar)
                    b = pr_alt[0] % 2
                    pr_alt[0] += 1
                    quads = [(whi[:, dc * 128:(dc + 1) * 128], wlo[:, dc * 128:(dc + 1) * 128],
                              xnh[:, dc * TT:(dc + 1) * TT], xnl[:, dc * TT:(dc + 1) * TT])
                             for dc in range(DC)]
                    ev_mm = mm3group(b, PB[b][:, 0:TT], quads, hold, qdeps=xn_qd)
                    wab_slots[bslot].append(ev_mm)
                    dst = dstbuf[:, oc * TT:(oc + 1) * TT]
                    ev_raw = rd_bank(b, vcopy(dst, PB[b][:, 0:TT],
                                              [ev_mm, attn_prev, comb_prev]))
                    if rope:
                        rb = 2 + (pr_alt[0] % 2)
                        d0 = list(bank_readers[rb]) + [ev_raw, ev_pmat]
                        bank_readers[rb] = []
                        ev_rot = mm(rb, PB[rb][:, 0:TT], pmat_s[:], dst, True, True, d0)
                        ev_t0 = rd_bank(rb, vtt(projscr[:, 0:TT], PB[rb][:, 0:TT],
                                                ropefm[:, TT:2 * TT], OP.mult,
                                                [ev_rot, ev_sin, ev_cos]))
                        ev_t1 = vtt(projscr[:, TT:2 * TT], dst, ropefm[:, 0:TT], OP.mult)
                        ev_rp = vtt(dst, projscr[:, TT:2 * TT], projscr[:, 0:TT], OP.add)
                        rope_cons.extend([ev_t0, ev_t1])
                        final = ev_rp
                    else:
                        final = ev_raw
                    finals.append(final)
                    if tmbuf is not None:
                        for s in range(NSUB):
                            b2 = 4 + (tr_alt[0] % 2)
                            tr_alt[0] += 1
                            evt = transpose(b2, PB[b2][:, 0:128],
                                            dstbuf[:, oc * TT + s * 128: oc * TT + s * 128 + 128],
                                            [final, attn_prev])
                            eve = scopy(tmbuf[:, s * 512 + oc * 128: s * 512 + oc * 128 + 128],
                                        PB[b2][:, 0:128], [evt, attn_prev])
                            rd_bank(b2, eve)
                            tm_last = eve
                return finals, tm_last

            ev_k, ev_ktm = proj(NKV, stk_d, hk_d, kfm, True, ktm, wm_kv, kv_bar[0])
            ev_v, ev_vtm = proj(NKV, stv_d, hv_d, vfm, False, vtm, wm_kv, kv_bar[0])
            if not prelude:
                ev_q, _ = proj(NH, stq_d, hq_d, qfm, True, None, wm_rest, rest_bar[0])

            ya_last = [None] * DC
            down_last_mm = [None]
            xn_last_mm_box = [None]
            gated_evs_sc = [[]]
            sg_prev = state["sg_last"]

            def emit_gateup_half(hgl, half):
                b = hgl % 2
                hcl = hgl % SCW
                if half == 0:
                    whi, wlo, hold, bslot = get_wpair(stg_d, hg_d, hgl, wm_rest, rest_bar[0])
                    quads = [(whi[:, dc * 128:(dc + 1) * 128], wlo[:, dc * 128:(dc + 1) * 128],
                              xnh[:, dc * TT:(dc + 1) * TT], xnl[:, dc * TT:(dc + 1) * TT])
                             for dc in range(DC)]
                    ev_gm = mm3group(b, PB[b][:, 0:TT], quads, hold, qdeps=xn_qd)
                    wab_slots[bslot].append(ev_gm)
                    emit_gateup_half.gm = ev_gm
                else:
                    ev_gm = emit_gateup_half.gm
                    whi, wlo, hold, bslot = get_wpair(stu_d, hu_d, hgl, wm_rest, rest_bar[0])
                    quads = [(whi[:, dc * 128:(dc + 1) * 128], wlo[:, dc * 128:(dc + 1) * 128],
                              xnh[:, dc * TT:(dc + 1) * TT], xnl[:, dc * TT:(dc + 1) * TT])
                             for dc in range(DC)]
                    d0 = list(hold)
                    ev_um = None
                    for i, (a1, a2, a3, a4) in enumerate(quads):
                        di = (d0 if i == 0 else []) + xn_qd[i]
                        ev_um = mm(b, PB[b][:, TT:2 * TT], a1, a3, i == 0, False, di)
                        ev_um = mm(b, PB[b][:, TT:2 * TT], a1, a4, False, False, ())
                        ev_um = mm(b, PB[b][:, TT:2 * TT], a2, a3, False, i == DC - 1, ())
                    wab_slots[bslot].append(ev_um)
                    xn_last_mm_box[0] = ev_um
                    fsl = hgl % 2
                    ev_si = rd_bank(b, sact(ffnscr[:, fsl * TT:(fsl + 1) * TT],
                                            PB[b][:, 0:TT], AF.Silu,
                                            [ev_gm, ev_um, ffn_reader[fsl]]))
                    ev_pr = rd_bank(b, vtt(projscr[:, fsl * TT:(fsl + 1) * TT],
                                           ffnscr[:, fsl * TT:(fsl + 1) * TT],
                                           PB[b][:, TT:2 * TT], OP.mult,
                                           [ev_si, down_last_mm[0], prod_reader[fsl]]))
                    co = hcl * TT
                    ev_gh = sact(gth[:, co:co + TT], projscr[:, fsl * TT:(fsl + 1) * TT],
                                 AF.Copy, deps=[ev_pr, down_last_mm[0]])
                    ev_gl = P.ev("gpsimd", (lambda eng, d=gtl[:, co:co + TT],
                                            a=projscr[:, fsl * TT:(fsl + 1) * TT],
                                            b2=gth[:, co:co + TT]:
                                            eng.tensor_tensor(d, a, b2, OP.subtract)),
                                 [ev_gh, down_last_mm[0]])
                    prod_reader[fsl] = ev_gl
                    ffn_reader[fsl] = ev_pr
                    gated_evs_sc[0].extend([ev_gh, ev_gl])

            def emit_down_sc(sc):
                gevs = gated_evs_sc[0]
                gqd = [[gevs[2 * h], gevs[2 * h + 1]] for h in range(SCW)]
                last = None
                for oc in range(DC):
                    dhi, dlo, hold, bslot = get_dpair(sc, oc, wm_rest, rest_bar[0])
                    yb = 4 + oc % 2
                    quads = [(dhi[:, hcl * 128:(hcl + 1) * 128], dlo[:, hcl * 128:(hcl + 1) * 128],
                              gth[:, hcl * TT:(hcl + 1) * TT], gtl[:, hcl * TT:(hcl + 1) * TT])
                             for hcl in range(SCW)]
                    ev_ym = mm3group(yb, PB[yb][:, 0:TT], quads, hold, qdeps=gqd)
                    wdb_slots[bslot].append(ev_ym)
                    if sc == 0:
                        ev_ya = scopy(yacc[:, oc * TT:(oc + 1) * TT], PB[yb][:, 0:TT],
                                      [ev_ym, sg_prev])
                    else:
                        ev_ya = vtt(yacc[:, oc * TT:(oc + 1) * TT],
                                    yacc[:, oc * TT:(oc + 1) * TT], PB[yb][:, 0:TT],
                                    OP.add, [ev_ym, ya_last[oc]])
                    rd_bank(yb, ev_ya)
                    ya_last[oc] = ev_ya
                    last = ev_ym
                down_last_mm[0] = last
                gated_evs_sc[0] = []

            attn_t_last = None
            o_evac_last = None
            steps = [(c, h) for c in range(NCHT) for h in range(NH)]
            stA = [None] * len(steps)
            for k in range(len(steps) + 1):
                if k < len(steps):
                    c, h = steps[k]
                    s, par = c // 2, c % 2
                    g = h // GRP
                    idx = k % 2
                    ab = 6 + idx
                    rec = {}
                    if not prelude:
                        d0 = list(bank_readers[ab]) + [ev_k[g], ev_q[h]]
                        bank_readers[ab] = []
                        rec["sc"] = mm(ab, PB[ab][:, 128:192],
                                       kfm[:, g * TT + s * 128: g * TT + s * 128 + 128],
                                       qfm[:, h * TT + c * 64: h * TT + c * 64 + 64],
                                       True, True, d0)
                        rec["st"] = rd_bank(ab, vtt(sTscr[:, idx * 64:(idx + 1) * 64],
                                                    PB[ab][:, 128:192],
                                                    maskt_s[:, par * 1024 + h * 64: par * 1024 + h * 64 + 64],
                                                    OP.mult, [rec["sc"], ev_maskt, st_reader[idx]]))
                        rec["qt"] = vtt(qtscr[:, idx * 64:(idx + 1) * 64],
                                        qfm[:, h * TT + c * 64: h * TT + c * 64 + 64],
                                        qdecb_s[:, h * 64:(h + 1) * 64], OP.mult,
                                        [ev_q[h], ev_qdecb, qt_reader[idx]])
                    rec["kt"] = vts(ktscr[:, idx * 128:(idx + 1) * 128],
                                    ktm[:, s * 512 + g * 128: s * 512 + g * 128 + 128],
                                    kdecx_s[:, par * 16 + h: par * 16 + h + 1], None,
                                    OP.mult, None, [ev_ktm, ev_kdecx, kt_reader[idx]])
                    stA[k] = rec
                if k == 0:
                    continue
                kb = k - 1
                c, h = steps[kb]
                s, par = c // 2, c % 2
                g = h // GRP
                idx = kb % 2
                ab = 6 + idx
                rec = stA[kb]
                if not prelude:
                    emit_gateup_half(kb, 0)
                    ob = 2 + idx
                    d1 = list(bank_readers[ob]) + [rec["st"], ev_vtm]
                    bank_readers[ob] = []
                    ev_o1 = mm(ob, PB[ob][:, 0:64],
                               vtm[:, s * 512 + g * 128: s * 512 + g * 128 + 128],
                               sTscr[:, idx * 64:(idx + 1) * 64], True, False, d1)
                    ev_o2 = mm(ob, PB[ob][:, 0:64],
                               kv_s[:, h * 128:(h + 1) * 128],
                               qtscr[:, idx * 64:(idx + 1) * 64], False, True,
                               [rec["qt"], ev_kv[h]])
                    co2 = h * TT + c * 64
                    ev_oh = scopy(oh[:, co2:co2 + 64], PB[ob][:, 0:64], [ev_o2, sg_prev])
                    ev_ol = vtt(ol[:, co2:co2 + 64], PB[ob][:, 0:64], oh[:, co2:co2 + 64],
                                OP.subtract, [ev_oh, sg_prev])
                    rd_bank(ob, ev_oh)
                    rd_bank(ob, ev_ol)
                    st_reader[idx] = ev_o1
                    qt_reader[idx] = ev_o2
                    o_evac_last = [ev_oh, ev_ol]
                    emit_gateup_half(kb, 1)
                d2 = list(bank_readers[ab]) + [rec["kt"], ev_vtm]
                if not prelude:
                    d2.append(rec["st"])
                bank_readers[ab] = []
                ev_A = mm(ab, PB[ab][:, 0:128],
                          ktscr[:, idx * 128:(idx + 1) * 128],
                          vtm[:, s * 512 + g * 128: s * 512 + g * 128 + 128],
                          True, True, d2)
                kvdeps = [ev_A, ev_kv[h], ev_blkb]
                if not prelude:
                    kvdeps.append(ev_o2)
                ev_kvu = rd_bank(ab, vstt(kv_s[:, h * 128:(h + 1) * 128],
                                          kv_s[:, h * 128:(h + 1) * 128],
                                          blkb_s[:, h:h + 1], PB[ab][:, 0:128],
                                          OP.mult, OP.add, kvdeps))
                ev_kv[h] = ev_kvu
                kt_reader[idx] = ev_A
                attn_t_last = ev_A
                if not prelude and kb % SCW == SCW - 1:
                    emit_down_sc(kb // SCW)
            state["attn_t_last"] = attn_t_last

            if prelude:
                state["xnfm_consumed"] = attn_t_last
                return

            ev_s = []
            for oc in range(DC):
                whi, wlo, hold, bslot = get_wpair(sto_d, ho_d, oc, wm_rest, rest_bar[0])
                b = pr_alt[0] % 2
                pr_alt[0] += 1
                quads = [(whi[:, ec * 128:(ec + 1) * 128], wlo[:, ec * 128:(ec + 1) * 128],
                          oh[:, ec * TT:(ec + 1) * TT], ol[:, ec * TT:(ec + 1) * TT])
                         for ec in range(DC)]
                ev_mm = mm3group(b, PB[b][:, 0:TT], quads, hold + o_evac_last)
                wab_slots[bslot].append(ev_mm)
                ev_se = rd_bank(b, vcopy(qfm[:, oc * TT:(oc + 1) * TT], PB[b][:, 0:TT],
                                         [ev_mm, attn_t_last]))
                ev_s.append(ev_se)

            state["xnfm_consumed"] = xn_last_mm_box[0]

            ev_xr = []
            for s in range(NSUB):
                ev_xr.append(dma(xbuf[:, s * D:(s + 1) * D],
                                 xsrc[t0 + s * 128: t0 + s * 128 + 128, :],
                                 f"xb{s}", xbuf_free[s]))
            sg_last = None
            add_last = [None, None]
            for oc in range(DC):
                ev_g = sact(yacc[:, oc * TT:(oc + 1) * TT], yacc[:, oc * TT:(oc + 1) * TT],
                            AF.Silu, [ya_last[oc]])
                ev_sg = vtt(qfm[:, oc * TT:(oc + 1) * TT], qfm[:, oc * TT:(oc + 1) * TT],
                            yacc[:, oc * TT:(oc + 1) * TT], OP.mult, [ev_g, ev_s[oc]])
                sg_last = ev_sg
                for s in range(NSUB):
                    b = 4 + (tr_alt[0] % 2)
                    tr_alt[0] += 1
                    evt = transpose(b, PB[b][:, 0:128],
                                    qfm[:, oc * TT + s * 128: oc * TT + s * 128 + 128],
                                    [ev_sg])
                    ev_add = rd_bank(b, vtt(xbuf[:, s * D + oc * 128: s * D + oc * 128 + 128],
                                            xbuf[:, s * D + oc * 128: s * D + oc * 128 + 128],
                                            PB[b][:, 0:128], OP.add, [evt, ev_xr[s]]))
                    add_last[s] = ev_add
                    state["comb_t_last"] = evt
            state["sg_last"] = sg_last
            for s in range(NSUB):
                ev_out = gdma(y_d[t0 + s * 128: t0 + s * 128 + 128, :],
                              xbuf[:, s * D:(s + 1) * D], f"o{s}", [add_last[s]])
                xbuf_free[s] = [ev_out]

        for ti in range(NTILE):
            do_tile(ti, prelude=True)
            if ti == 0:
                kv_bar[0] = [store_last[k] for k in ("swa0", "swa1")
                             if k in store_last]
        for ti in range(NTILE):
            do_tile(ti, prelude=False)
            if ti == 0:
                rest_bar[0] = [store_last[k] for k in
                               ("swa0", "swa1", "swd0", "swd1") if k in store_last]

        P.finalize()
        block = E(nc.Block())

        @block.sync
        def _(eng):
            P.run("sync", eng, sems)

        @block.tensor
        def _(eng):
            P.run("tensor", eng, sems)

        @block.vector
        def _(eng):
            P.run("vector", eng, sems)

        @block.scalar
        def _(eng):
            P.run("scalar", eng, sems)

        @block.gpsimd
        def _(eng):
            P.run("gpsimd", eng, sems)
            eng.wait_ge(sems["D_o0"], P.dma_n["o0"] * 16)
            eng.wait_ge(sems["D_o1"], P.dma_n["o1"] * 16)

    return nc


def _tables():
    import jax
    import jax.numpy as jnp
    cpu = jax.devices("cpu")[0]
    with jax.default_device(cpu):
        i = jnp.arange(1, NH + 1, dtype=jnp.float32)
        lam = jnp.exp2(-8.0 * i / NH)
        r = jnp.arange(64, dtype=jnp.float32)
        diff = r[:, None] - r[None, :]
        diag = jnp.where(diff >= 0, jnp.exp(-lam[:, None, None] * diff), 0.0)
        q_dec = jnp.exp(-lam[:, None] * (r + 1.0))
        k_dec = jnp.exp(-lam[:, None] * (64 - 1.0 - r))
        blk = jnp.exp(-lam * 64)
        inv_freq = 1.0 / 10000.0 ** (jnp.arange(0, DH, 2, dtype=jnp.float32) / DH)
        freqs = jnp.arange(4096, dtype=jnp.float32)[:, None] * inv_freq[None, :]
        cos = jnp.concatenate([jnp.cos(freqs), jnp.cos(freqs)], axis=-1)
        sin = jnp.concatenate([jnp.sin(freqs), jnp.sin(freqs)], axis=-1)
        out = [np.asarray(t) for t in (diag, q_dec, k_dec, blk, cos, sin)]
    return out


def _retile(w, nout):
    no, din = w.shape
    assert no == nout * 128
    ndc = din // 128
    return np.ascontiguousarray(
        w.reshape(nout, 128, ndc, 128).transpose(0, 3, 2, 1)).reshape(nout, 128, din)


_NC_CACHE = {}


def kernel(L0_attn_x, L0_attn_wq, L0_attn_wk, L0_attn_wv, L0_attn_wo,
           L0_ffn_w_gate, L0_ffn_w_up, L0_ffn_w_down):
    x = np.ascontiguousarray(np.asarray(L0_attn_x, dtype=np.float32)).reshape(8 * TOKC, D)
    wq = np.asarray(L0_attn_wq, dtype=np.float32)
    wk = np.asarray(L0_attn_wk, dtype=np.float32)
    wv = np.asarray(L0_attn_wv, dtype=np.float32)
    wo = np.asarray(L0_attn_wo, dtype=np.float32)
    wg = np.asarray(L0_ffn_w_gate, dtype=np.float32)
    wu = np.asarray(L0_ffn_w_up, dtype=np.float32)
    wdn = np.asarray(L0_ffn_w_down, dtype=np.float32)

    stq = _retile(wq, NH)
    stk = _retile(wk, NKV)
    stv = _retile(wv, NKV)
    sto = _retile(wo, DC)
    stg = _retile(wg, HC)
    stu = _retile(wu, HC)
    stdw = np.ascontiguousarray(
        wdn.reshape(DC, 128, NSC, SCW, 128).transpose(2, 0, 4, 3, 1)
    ).reshape(NSC, DC, 128, SCW * 128)

    diag, q_dec, k_dec, blk, cos, sin = _tables()
    maskt = np.zeros((128, 2048), np.float32)
    for par in range(2):
        for h in range(NH):
            maskt[par * 64:(par + 1) * 64, par * 1024 + h * 64: par * 1024 + (h + 1) * 64] = diag[h].T
    qdecb = np.zeros((128, 1024), np.float32)
    for h in range(NH):
        qdecb[:, h * 64:(h + 1) * 64] = q_dec[h][None, :]
    kdecx = np.zeros((128, 32), np.float32)
    for par in range(2):
        for h in range(NH):
            kdecx[par * 64:(par + 1) * 64, par * 16 + h] = k_dec[h]
    blkb = np.broadcast_to(blk[None, :], (128, 16)).astype(np.float32).copy()
    idm = np.eye(128, dtype=np.float32)
    pmat = np.zeros((128, 128), np.float32)
    for d in range(64):
        pmat[d, d + 64] = 1.0
        pmat[d + 64, d] = -1.0
    cosp = np.ascontiguousarray(cos[0:TOKC].T)
    sinp = np.ascontiguousarray(sin[0:TOKC].T)

    if "nc" not in _NC_CACHE:
        _NC_CACHE["nc"] = build_nc()
    nc = _NC_CACHE["nc"]

    zeros_x = np.zeros((TOKC, D), np.float32)
    in_maps = []
    for c in range(8):
        half = c % 2
        b = c // 2
        xm = x[c * TOKC:(c + 1) * TOKC]
        xpm = x[b * 2 * TOKC: b * 2 * TOKC + TOKC] if half == 1 else zeros_x
        p0 = half * TOKC
        in_maps.append(dict(
            x=xm, xp=xpm, stq=stq, stk=stk, stv=stv, sto=sto, stg=stg, stu=stu,
            stdw=stdw, maskt=maskt, qdecb=qdecb, kdecx=kdecx, blkb=blkb,
            cosm=np.ascontiguousarray(cos[p0:p0 + TOKC].T),
            sinm=np.ascontiguousarray(sin[p0:p0 + TOKC].T),
            cosp=cosp, sinp=sinp, idm=idm, pmat=pmat,
            epsc=np.full((128, 1), 1e-5, np.float32),
        ))
    res = run_bass_kernel_spmd(nc, in_maps, core_ids=list(range(8)))
    y = np.concatenate([np.asarray(res.results[c]["y"]) for c in range(8)], axis=0)
    return y.reshape(4, 4096, 2048).astype(np.float32)


if __name__ == "__main__":
    nc = build_nc()
    print("built ok")
